# version 17
# baseline (speedup 1.0000x reference)
import math
import numpy as np
import ml_dtypes
import concourse.bass as bass
import concourse.mybir as mybir
from concourse.bass_utils import run_bass_kernel_spmd

F32 = mybir.dt.float32
BF16 = mybir.dt.bfloat16
AF = mybir.ActivationFunctionType
ALU = mybir.AluOpType
AX = mybir.AxisListType
BF = ml_dtypes.bfloat16

T = 2048
D = 1024
NCORE = 8
NBLOCKS = 4
RN_C = 12582912.0
TWO_PI = 2.0 * math.pi

ENGS = ("t", "v", "s", "g", "y")
ENG_ATTR = {"t": "tensor", "v": "vector", "s": "scalar", "g": "gpsimd", "y": "sync"}


class Prog:

    def __init__(self, nc):
        self.nc = nc
        self.recs = {e: [] for e in ENGS}
        self.sems = {}
        self.c = {}
        self._sem_cms = []
        self.bank_free = {}

    def sem(self, name):
        cm = self.nc.semaphore(name)
        h = cm.__enter__()
        self._sem_cms.append(cm)
        self.sems[name] = h
        self.c[name] = 0
        return h

    def emit(self, eng, method, *args, wait=None, waits=None, inc=None, incn=1, **kw):
        ws = list(waits) if waits else []
        if wait is not None:
            ws.append(wait)
        if inc is not None:
            self.c[inc] += incn
        self.recs[eng].append((method, args, kw, ws, inc, incn))

    def w(self, name):
        return (name, self.c[name])

    def bank_wait(self, bank):
        f = self.bank_free.pop(bank, None)
        return [f] if f else []

    def bank_used(self, bank, sem_val):
        self.bank_free[bank] = sem_val

    def barrier(self):
        tgt = self.c["bar"] + len(ENGS)
        for e in ENGS:
            self.emit(e, None, inc="bar")
        for e in ENGS:
            self.recs[e].append((None, (), {}, [("bar", tgt)], None, 1))
        self.bank_free.clear()

    def build(self):
        with self.nc.Block() as block:
            for eng in ENGS:
                recs = self.recs[eng]

                def body(E, recs=recs):
                    for method, args, kw, ws, inc, incn in recs:
                        for (sn, sv) in ws:
                            E.wait_ge(self.sems[sn], sv)
                        ins = None
                        if method is not None:
                            ins = getattr(E, method)(*args, **kw)
                        if inc is not None:
                            if ins is None:
                                ins = E.nop()
                            ins.then_inc(self.sems[inc], incn)
                getattr(block, ENG_ATTR[eng])(body)
        for cm in reversed(self._sem_cms):
            cm.__exit__(None, None, None)


def make_consts():
    c = {}

    c["ones1b"] = np.ones((1, 128), BF)
    c["colb"] = np.ones((128, 1), BF)
    c["colf"] = np.ones((128, 1), np.float32)
    c["idb"] = np.eye(128, dtype=np.float32).astype(BF)
    sw = np.zeros((128, 128), np.float32)
    for k in range(128):
        sw[k, k ^ 1] = 1.0
    c["swapb"] = sw.astype(BF)
    c["scanm"] = np.broadcast_to((np.arange(512) % 64 != 0).astype(np.float32),
                                 (128, 512)).astype(BF).copy()
    ROPE_BASE, DR = 10000.0, 64
    inv_freq = 1.0 / ROPE_BASE ** (np.arange(0, DR, 2, dtype=np.float32) / DR)
    tt = np.arange(T, dtype=np.float32)
    freqs = np.outer(tt, inv_freq)
    cos = np.concatenate([np.cos(freqs), np.cos(freqs)], -1)[:, ::2]
    sin = np.concatenate([np.sin(freqs), np.sin(freqs)], -1)[:, ::2]
    ropeC = np.zeros((128, T), np.float32)
    ropeS = np.zeros((128, T), np.float32)
    for i in range(32):
        ropeC[2 * i] = cos[:, i]
        ropeC[2 * i + 1] = cos[:, i]
        ropeS[2 * i] = -sin[:, i]
        ropeS[2 * i + 1] = sin[:, i]
    ropeC[64:] = ropeC[:64]
    ropeS[64:] = ropeS[:64]
    c["ropeC"] = ropeC.astype(BF)
    c["ropeS"] = ropeS.astype(BF)
    c["iota8"] = np.broadcast_to(np.arange(8, dtype=np.float32), (128, 8)).copy()
    c["idf"] = np.eye(16, dtype=np.float32)
    c["onesall"] = np.ones((65, 128), np.float32)
    e16 = np.zeros((128, 128), np.float32)
    for i in range(8):
        e16[0:64, 16 * i + 2 * i] = 1.0
        e16[64:128, 16 * i + 2 * i + 1] = 1.0
    c["e16"] = e16.astype(BF)
    return c


CONSTS = make_consts()

WSHAPES = {
    "L0_attn_Wk": (D, D), "L0_attn_Wv": (D, D), "L0_attn_Wq": (D, D),
    "L0_attn_Wkp": (D, D), "L0_attn_Wqp": (D, D), "L0_attn_Wo": (D, D),
    "L0_ffn_Wr": (D, 8),
    "L0_ffn_W1": (8 * 1024, 2048), "L0_ffn_W2": (8 * 2048, 1024),
    "L1_attn_w_down_q": (512, D), "L1_attn_w_down_kv": (512, D),
    "L1_attn_w_up_q": (D, 512), "L1_attn_w_up_k": (D, 512),
    "L1_attn_w_up_v": (D, 512), "L1_attn_w_out": (D, D),
    "L1_ffn_w1_w": (4096, D), "L1_ffn_w2_w": (4096, D),
    "L1_ffn_h1_w": (4096, 4096), "L1_ffn_h2_w": (4096, 4096),
    "L1_ffn_w3_w": (D, 4096),
    "L0_ffn_b1": (8, 2048), "L0_ffn_b2": (8, 1024),
}
VSHAPES = {
    "L0_attn_bk": D, "L0_attn_bv": D, "L0_attn_bq": D, "L0_attn_bkp": D,
    "L0_attn_bqp": D, "L0_attn_phase_scale": D, "L0_attn_ln_g": D,
    "L0_attn_ln_b": D, "L0_attn_bo": D, "L0_ffn_br": 8,
    "L1_attn_ln_scale": D, "L1_attn_ln_bias": D,
    "L1_ffn_w1_s": 4096, "L1_ffn_w2_s": 4096, "L1_ffn_h1_s": 4096,
    "L1_ffn_h2_s": 4096, "L1_ffn_w3_s": D,
}


def build_program():
    nc = bass.Bass("TRN2", target_bir_lowering=False)
    P = Prog(nc)

    dr = {}
    dr["xT"] = nc.dram_tensor("xT", [D, T], F32, kind="ExternalInput")
    dr["out"] = nc.dram_tensor("outT", [D, T], F32, kind="ExternalOutput")
    dr["R"] = nc.dram_tensor("Rscratch", [D, T], F32)
    for n, sh in WSHAPES.items():
        dr[n] = nc.dram_tensor(n, list(sh), F32, kind="ExternalInput")
    for n, ln in VSHAPES.items():
        dr[n] = nc.dram_tensor(n, [ln], F32, kind="ExternalInput")
    for n, a in CONSTS.items():
        dr[n] = nc.dram_tensor("c_" + n, list(a.shape), BF16 if a.dtype == BF else F32,
                               kind="ExternalInput")

    sba = nc.alloc_sbuf_tensor
    BIG1 = sba("BIG1", [128, 32768], BF16)
    BIG2 = sba("BIG2", [128, 32768], BF16)
    BIG3 = sba("BIG3", [128, 20480], BF16)
    RS = sba("RS", [128, 4096], F32)
    STAT = sba("STAT", [128, 2560], F32)
    SCRF = sba("SCRF", [128, 1024], F32)
    SQB = SCRF.bitcast(BF16)
    ROPE2 = sba("ROPE2", [128, 1024], BF16)
    QKM = sba("QKM", [128, 8], F32)
    LNP = sba("LNP", [128, 1024], BF16)
    LNPF = LNP.bitcast(F32)
    B2E = LNP
    CS = {}
    for n, a in CONSTS.items():
        if n in ("ropeC", "ropeS"):
            continue
        CS[n] = sba(n, list(a.shape), BF16 if a.dtype == BF else F32)
    PVX = sba("PVX", [128, 96], F32)
    BRROW = sba("BRROW", [1, 8], F32)
    GB8 = sba("GB8", [128, 16], F32)

    ps = nc.alloc_psum_tensor
    PSN = [ps(f"PS{i}", [128, 512], F32) for i in range(6)]
    PT = ps("PT", [128, 1024], BF16)
    PT2 = ps("PT2", [128, 1024], BF16)
    PTS = [PT, PT2]
    PTS4 = [(PT, "PT"), (PT2, "PT2"),
            (PSN[4].bitcast(BF16), 4), (PSN[5].bitcast(BF16), 5)]
    PREP_GI = {"i": 0}

    for s in ("wl", "hio", "ts", "vs", "ss", "gs", "bar", "wn0", "wn1",
              "wa0", "wa1", "wb0", "wb1", "wr2", "msn", "lnp0", "lnp1", "wpre",
              "rio", "hes", "hfs"):
        P.sem(s)

    def Rsl(i):
        return RS[:, i * 512:(i + 1) * 512]

    def gload(dst_ap, src_ap, waits=None, slow=False, sem="wl"):
        kw = {"allow_slow_non_contiguous": True} if slow else {}
        P.emit("g", "dma_start", dst_ap, src_ap, waits=waits, inc=sem, incn=16, **kw)
        return P.w(sem)

    def hdma(dst_ap, src_ap, waits=None, sem="hio"):
        P.emit("y", "dma_start", dst_ap, src_ap, waits=waits, inc=sem, incn=16)
        return P.w(sem)

    for n in CONSTS:
        if n in ("ropeC", "ropeS"):
            continue
        gload(CS[n][:, :], dr[n][:, :])
    VCOL = {}
    ncol = 0
    for n, short in (("L0_attn_bk", "bk"), ("L0_attn_bv", "bv"), ("L0_attn_bq", "bq"),
                     ("L0_attn_bkp", "bkp"), ("L0_attn_bqp", "bqp"),
                     ("L0_attn_phase_scale", "ps"), ("L0_attn_ln_g", "lng"),
                     ("L0_attn_ln_b", "lnb"), ("L0_attn_bo", "bo"),
                     ("L1_attn_ln_scale", "l1s"), ("L1_attn_ln_bias", "l1b")):
        ap = bass.AP(dr[n], 0, [[1, 128], [128, 8]])
        w_pv = gload(PVX[:, ncol:ncol + 8], ap, slow=True)
        VCOL[short] = ncol
        ncol += 8
    VCOL["ps2"] = ncol
    ncol += 8

    def pv(short, m):
        return PVX[:, VCOL[short] + m: VCOL[short] + m + 1]

    WR32 = sba("WR32", [128, 64], F32)
    for i in range(8):
        w_wr = gload(WR32[:, i * 8:(i + 1) * 8], dr["L0_ffn_Wr"][i * 128:(i + 1) * 128, :])
    B1C = sba("B1C", [128, 128], F32)
    for e in range(8):
        ap = bass.AP(dr["L0_ffn_b1"], 2048 * e, [[1, 128], [128, 16]])
        gload(B1C[:, e * 16:(e + 1) * 16], ap, slow=True)
    w_br = gload(BRROW[:, :], bass.AP(dr["L0_ffn_br"], 0, [[1, 1], [1, 8]]))
    SCL = sba("SCL", [128, 144], F32)
    SCOL = {}
    off = 0
    for n, nch in (("L1_ffn_w1_s", 32), ("L1_ffn_w2_s", 32), ("L1_ffn_h1_s", 32),
                   ("L1_ffn_h2_s", 32), ("L1_ffn_w3_s", 8)):
        ap = bass.AP(dr[n], 0, [[1, 128], [128, nch]])
        gload(SCL[:, off:off + nch], ap, slow=True)
        SCOL[n] = off
        off += nch

    def scol(name, m):
        return SCL[:, SCOL[name] + m: SCOL[name] + m + 1]

    consts_w = P.w("wl")
    P.emit("v", "tensor_scalar_mul", PVX[:, VCOL["ps2"]:VCOL["ps2"] + 8],
           PVX[:, VCOL["ps"]:VCOL["ps"] + 8], 1.0 / TWO_PI, wait=consts_w, inc="vs")
    P.emit("v", "drain")
    P.emit("t", None, wait=consts_w)
    P.emit("s", None, wait=consts_w)

    def layer_norm(src, dst, cslot, eps, add_to=None, fp32_stats=False,
                   gb=None, src_is_bf=False, src_wait=None, rowbase=0):
        bS, bQ = 3, 4
        if src_wait is None:
            swf = lambda i: None
        elif callable(src_wait):
            swf = src_wait
        else:
            swf = lambda i, s=src_wait: (s if i == 0 else None)
        twS = P.bank_wait(bS)
        twQ = P.bank_wait(bQ)
        cast_w = [None] * 8
        for i in range(8):
            if fp32_stats:
                P.emit("t", "matmul", PSN[bS][0:1, 0:512], CS["colf"][:, :], src(i),
                       start=(i == 0), stop=(i == 7),
                       waits=(twS if i == 0 else []) +
                             ([swf(i)] if swf(i) else []),
                       inc="ts" if i == 7 else None)
            else:
                xb = src(i) if src_is_bf else cslot(i)
                if not src_is_bf:
                    P.emit("v", "tensor_copy", xb, src(i),
                           waits=[swf(i)] if swf(i) else [],
                           inc="vs")
                    P.emit("v", "drain")
                    cast_w[i] = P.w("vs")
                P.emit("t", "matmul", PSN[bS][0:1, 0:512], CS["colb"][:, :], xb,
                       start=(i == 0), stop=(i == 7),
                       waits=([P.w("vs")] if not src_is_bf else []) +
                             (twS if i == 0 else []) +
                             ([swf(i)] if (src_is_bf and swf(i)) else []),
                       inc="ts" if i == 7 else None)
        sum_done = P.w("ts")
        for i in range(8):
            xb = src(i) if (src_is_bf or fp32_stats) else cslot(i)
            sq = SQB[:, (i % 2) * 512:(i % 2) * 512 + 512] if not fp32_stats else \
                SCRF[:, (i % 2) * 512:(i % 2) * 512 + 512]
            sqw = []
            if cast_w[i] is not None:
                sqw.append(cast_w[i])
            elif swf(i):
                sqw.append(swf(i))
            if i >= 2:
                sqw.append(P.w("ts"))
            P.emit("s", "activation", sq, xb, AF.Square, waits=sqw, inc="ss")
            if fp32_stats:
                P.emit("t", "matmul", PSN[bQ][0:1, 0:512], CS["colf"][:, :], sq,
                       start=(i == 0), stop=(i == 7),
                       waits=[P.w("ss")] + (twQ if i == 0 else []), inc="ts")
            else:
                P.emit("t", "matmul", PSN[bQ][0:1, 0:512], CS["colb"][:, :], sq,
                       start=(i == 0), stop=(i == 7),
                       waits=[P.w("ss")] + (twQ if i == 0 else []), inc="ts")
        sq_done = P.w("ts")
        MU = STAT[0:1, rowbase:rowbase + 512]
        RSr = STAT[0:1, rowbase + 512:rowbase + 1024]
        P.emit("v", "tensor_scalar_mul", MU, PSN[bS][0:1, 0:512],
               1.0 / 1024.0, waits=[sum_done, sq_done], inc="vs")
        P.emit("v", "drain")
        P.emit("v", "tensor_scalar_mul", PSN[bQ][0:1, 0:512], PSN[bQ][0:1, 0:512],
               1.0 / 1024.0)
        P.emit("v", "drain")
        P.emit("v", "tensor_tensor", RSr, MU, MU, ALU.mult)
        P.emit("v", "drain")
        P.emit("v", "tensor_sub", PSN[bQ][0:1, 0:512], PSN[bQ][0:1, 0:512], RSr)
        P.emit("v", "drain")
        P.emit("v", "tensor_scalar_add", PSN[bQ][0:1, 0:512], PSN[bQ][0:1, 0:512],
               float(eps))
        P.emit("v", "drain")
        P.emit("v", "engine_nop", inc="vs")
        P.emit("s", "activation", RSr, PSN[bQ][0:1, 0:512], AF.Sqrt,
               wait=P.w("vs"), inc="ss")
        P.emit("v", "reciprocal", RSr, RSr, wait=P.w("ss"), inc="vs")
        P.emit("v", "drain")
        rows_done = P.w("vs")
        P.emit("t", "matmul", PSN[bS][:, 0:512], CS["onesall"][0:1, :], MU,
               start=True, stop=True, waits=[rows_done], inc="ts")
        P.emit("t", "matmul", PSN[bQ][:, 0:512], CS["onesall"][0:1, :], RSr,
               start=True, stop=True, inc="ts")
        bc_done = P.w("ts")
        for i in range(8):
            tmp = SCRF[:, 0:512]
            P.emit("v", "tensor_sub", tmp, src(i), PSN[bS][:, 0:512],
                   waits=[bc_done] if i == 0 else [])
            P.emit("v", "drain")
            if add_to is not None:
                ln32 = SCRF[:, 512:1024]
                P.emit("v", "tensor_mul", ln32, tmp, PSN[bQ][:, 0:512])
                P.emit("v", "drain")
                P.emit("v", "tensor_add", add_to(i), add_to(i), ln32)
                P.emit("v", "drain")
                P.emit("v", "tensor_copy", dst(i), ln32,
                       inc="vs" if (i == 7 and gb is None) else None)
                P.emit("v", "drain")
            else:
                P.emit("v", "tensor_mul", dst(i), tmp, PSN[bQ][:, 0:512],
                       inc="vs" if (i == 7 and gb is None) else None)
                P.emit("v", "drain")
        if gb is None:
            dw = P.w("vs")
            P.bank_used(bS, dw)
            P.bank_used(bQ, dw)
            return dw, dw
        P.emit("v", "engine_nop", inc="vs")
        nv = P.w("vs")
        P.bank_used(bS, nv)
        P.bank_used(bQ, nv)
        sc, bc = gb
        for i in range(8):
            P.emit("s", "activation", dst(i), dst(i), AF.Identity,
                   scale=sc(i), bias=bc(i), waits=[nv] if i == 0 else [],
                   inc="ss" if i == 7 else None)
            P.emit("s", "drain")
        dw = P.w("ss")
        return dw, nv

    PT2F = PT2.bitcast(F32)
    PTF0 = PT.bitcast(F32)
    LNP_FREE = {"w": []}

    def ln_prefetch(src_dram, ts0, slot, fp32=False, rs2_row=None, eps=1e-5,
                    entry=None, banks=None):
        if banks is None:
            banks = (("PT2", PT2F), ("PT", PT.bitcast(F32)))
        (bkS, apS), (bkQ, apQ) = banks
        state = {"dma": {}}

        assert not fp32, "fp32 prefetch unsupported (slot budget)"

        def chunk_dma(i, w):
            sl_ = (i // 2) % 2
            state["dma"][i] = state["dma"][i + 1] = gload(
                LNP[:, 0:1024],
                bass.AP(src_dram, ts0 + (i * 128) * 2048,
                        [[2048, 128], [128 * 2048, 2], [1, 512]]),
                sem=f"lnp{sl_}", waits=(w or []) + LNP_FREE["w"])

        def dma_issue():
            chunk_dma(0, [entry] if entry else None)

        def mms(row_free=None):
            MUr = STAT[0:1, slot * 1024: slot * 1024 + 512]
            RSr = STAT[0:1, slot * 1024 + 512: slot * 1024 + 1024]
            colv = CS["colf"] if fp32 else CS["colb"]
            twS = P.bank_wait(bkS)
            twQ = P.bank_wait(bkQ)
            for i in range(8):
                sl_ = i % 2
                xb = LNP[:, sl_ * 512:(sl_ + 1) * 512]
                P.emit("t", "matmul", apS[0:1, 0:512], colv[:, :], xb,
                       start=(i == 0), stop=(i == 7),
                       waits=(twS if i == 0 else []) + [state["dma"][i]],
                       inc="ts")
                mms_i = P.w("ts")
                P.emit("s", "activation", xb, xb, AF.Square,
                       waits=[mms_i], inc="ss")
                sq_done = P.w("ss")
                P.emit("t", "matmul", apQ[0:1, 0:512], colv[:, :], xb,
                       start=(i == 0), stop=(i == 7),
                       waits=[sq_done] + (twQ if i == 0 else []), inc="ts")
                mmq_i = P.w("ts")
                if i % 2 == 1 and i + 1 < 8:
                    chunk_dma(i + 1, [mmq_i, mms_i])
            P.emit("v", "tensor_scalar_mul", MUr, apS[0:1, 0:512], 1.0 / 1024.0,
                   waits=[P.w("ts")] + (row_free or []), inc="vs")
            P.emit("v", "drain")
            P.bank_used(bkS, P.w("vs"))
            P.emit("v", "tensor_scalar_mul", apQ[0:1, 0:512], apQ[0:1, 0:512],
                   1.0 / 1024.0)
            P.emit("v", "drain")
            P.emit("v", "tensor_tensor", RSr, MUr, MUr, ALU.mult)
            P.emit("v", "drain")
            P.emit("v", "tensor_sub", apQ[0:1, 0:512], apQ[0:1, 0:512], RSr)
            P.emit("v", "drain")
            if rs2_row is not None:
                P.emit("v", "tensor_scalar", rs2_row, apQ[0:1, 0:512], 1.0 + eps,
                       eps * eps, ALU.mult, ALU.add)
                P.emit("v", "drain")
                P.emit("v", "engine_nop", inc="vs")
                P.emit("s", "activation", rs2_row, rs2_row, AF.Sqrt,
                       wait=P.w("vs"), inc="ss")
                P.emit("v", "reciprocal", rs2_row, rs2_row, wait=P.w("ss"))
                P.emit("v", "drain")
            P.emit("v", "tensor_scalar_add", apQ[0:1, 0:512], apQ[0:1, 0:512],
                   float(eps))
            P.emit("v", "drain")
            P.emit("v", "engine_nop", inc="vs")
            P.emit("s", "activation", RSr, apQ[0:1, 0:512], AF.Sqrt,
                   wait=P.w("vs"), inc="ss")
            P.emit("v", "reciprocal", RSr, RSr, wait=P.w("ss"), inc="vs")
            P.emit("v", "drain")
            P.bank_used(bkQ, P.w("vs"))
            LNP_FREE["w"] = [P.w("ts")]
            return P.w("vs")

        return dma_issue, mms

    def ln_apply(slot, src, dst, rows_done, src_wait=None, add_to=None, gb=None,
                 rs2_row=None, dst2=None, gb2=None, banks=None):
        if banks is None:
            banks = ((4, PSN[4]), (5, PSN[5]), (3, PSN[3]))
        (kM, apM), (kR, apR), (kR2, apR2) = banks
        MUr = STAT[0:1, slot * 1024: slot * 1024 + 512]
        RSr = STAT[0:1, slot * 1024 + 512: slot * 1024 + 1024]
        tw4 = P.bank_wait(kM)
        tw5 = P.bank_wait(kR)
        P.emit("t", "matmul", apM[:, 0:512], CS["onesall"][0:1, :], MUr,
               start=True, stop=True, waits=tw4 + [rows_done], inc="ts")
        P.emit("t", "matmul", apR[:, 0:512], CS["onesall"][0:1, :], RSr,
               start=True, stop=True, waits=tw5, inc="ts")
        if rs2_row is not None:
            tw3 = P.bank_wait(kR2)
            P.emit("t", "matmul", apR2[:, 0:512], CS["onesall"][0:1, :], rs2_row,
                   start=True, stop=True, waits=tw3, inc="ts")
        bc = P.w("ts")
        ln_apply.last_bc = bc
        slotw = {}
        sw_l = ([] if src_wait is None else
                (list(src_wait) if isinstance(src_wait, list) else [src_wait]))
        for i in range(8):
            s = SCRF[:, (i % 2) * 512:(i % 2) * 512 + 512]
            ws = ([bc] if i == 0 else []) + (sw_l if i == 0 else []) + \
                 (slotw.get(i - 2, []))
            P.emit("v", "tensor_sub", s, src(i), apM[:, 0:512], waits=ws)
            P.emit("v", "drain")
            if rs2_row is not None:
                P.emit("v", "tensor_mul", dst2(i), s, apR2[:, 0:512])
                P.emit("v", "drain")
                P.emit("v", "engine_nop", inc="vs")
                sc, bcn = gb2
                P.emit("s", "activation", dst2(i), dst2(i), AF.Identity,
                       scale=sc(i), bias=bcn(i), wait=P.w("vs"), inc="ss")
                P.emit("s", "drain")
            P.emit("v", "tensor_mul", s, s, apR[:, 0:512], inc="vs")
            P.emit("v", "drain")
            mv = P.w("vs")
            myw = []
            if add_to is not None:
                P.emit("g", "tensor_add", add_to(i), add_to(i), s, wait=mv,
                       inc="gs")
                myw.append(P.w("gs"))
            if dst is not None:
                P.emit("s", "activation", dst(i), s,
                       AF.Identity if gb is None else AF.Identity,
                       scale=(gb[0](i) if gb else 1.0),
                       bias=(gb[1](i) if gb else 0.0), wait=mv, inc="ss")
                P.emit("s", "drain")
                myw.append(P.w("ss"))
            if not myw:
                myw = [mv]
            slotw[i] = list(myw)
        nv = P.w("vs")
        P.bank_used(kM, nv)
        P.bank_used(kR, nv)
        dst_w = P.w("ss") if (dst is not None or rs2_row is not None) else nv
        aux_w = P.w("gs") if add_to is not None else nv
        if add_to is not None:
            P.emit("v", None, wait=aux_w)
            P.emit("s", None, wait=aux_w)
        if dst is not None or rs2_row is not None:
            P.emit("v", None, wait=P.w("ss"))
        if rs2_row is not None:
            P.bank_used(kR2, nv)
        return dst_w, aux_w

    def prep_wT(dram_w, rows, cols, dst_pool, dst_off, natbuf, natw=None,
                evac="v", hist=None, nslots=2, sem=None, pre=None, deep=False):
        nr, ncch = rows // 128, cols // 128
        last = None
        if hist is None:
            hist = prep_hist
        for r in range(nr):
            if pre is not None:
                nslot = pre[0](r)[:, 0:cols]
                wl = pre[1] if r == 0 else None
            else:
                slot = len(hist) % nslots
                nslot = natbuf[slot][:, 0:cols]
                sm = sem[slot % 2] if isinstance(sem, tuple) else \
                    (sem or f"wn{slot % 2}")
                wl = gload(nslot, dram_w[r * 128:(r + 1) * 128, :],
                           sem=sm,
                           waits=[hist[-nslots]] if len(hist) >= nslots else None)
            for gi, g0 in enumerate(range(0, ncch, 8)):
                gn = min(8, ncch - g0)
                if deep:
                    ptile, pbk = PTS4[PREP_GI["i"] % 4]
                    PREP_GI["i"] += 1
                else:
                    pb = gi % 2
                    ptile = PTS[pb]
                    pbk = "PT" if pb == 0 else "PT2"
                tw = P.bank_wait(pbk)
                for j in range(gn):
                    c = g0 + j
                    wts = (tw if j == 0 else []) + \
                          ([wl] if (j == 0 and wl is not None) else [])
                    P.emit("t", "transpose", ptile[:, j * 128:(j + 1) * 128],
                           nslot[:, c * 128:(c + 1) * 128], CS["idb"][:, :],
                           waits=wts, inc="ts" if j == gn - 1 else None)
                tv = P.w("ts")
                esem = "vs" if evac == "v" else "ss"
                meth = "tensor_copy" if evac == "v" else "copy"
                if rows == 128:
                    o = dst_off + g0 * 128
                    P.emit(evac, meth, dst_pool[:, o:o + gn * 128],
                           ptile[:, 0:gn * 128], waits=[tv], inc=esem)
                else:
                    for j in range(gn):
                        c = g0 + j
                        o = dst_off + c * rows + r * 128
                        P.emit(evac, meth, dst_pool[:, o:o + 128],
                               ptile[:, j * 128:(j + 1) * 128],
                               waits=[tv] if j == 0 else [],
                               inc=esem if j == gn - 1 else None)
                if evac == "v":
                    P.emit("v", "drain")
                else:
                    P.emit("s", "drain")
                last = P.w(esem)
                P.bank_used(pbk, last)
            if pre is None:
                hist.append(P.w("ts"))
        return last

    prep_hist = []
    PRELD = {}

    def block_l0_attn(src_dram, dst_dram, entry_store=None):
        natb = [BIG3[:, 16384:17408], BIG3[:, 17408:18432]]
        pf = {}
        di0, mm0 = ln_prefetch(src_dram, 0, 0, entry=entry_store)
        di0()
        pf[0] = mm0()
        wR = hdma(RS[:, :],
                  bass.AP(src_dram, 0, [[2048, 128], [128 * 2048, 8], [1, 512]]),
                  waits=[entry_store] if entry_store else None)
        dstw, radd_w = ln_apply(0, lambda i: Rsl(i),
                                lambda i: BIG1[:, i * 512:(i + 1) * 512],
                                pf[0], src_wait=wR, add_to=lambda i: Rsl(i))
        es_w = hdma(bass.AP(dst_dram, 0, [[2048, 128], [128 * 2048, 8], [1, 512]]),
                    RS[:, :], waits=[radd_w], sem="hes")
        kq_g_free = None
        WT = {}
        WTW = {}
        for wi, (name, pool, offc) in enumerate((("Wq", BIG2, 0), ("Wqp", BIG2, 24576),
                                                 ("Wk", BIG2, 8192), ("Wkp", BIG2, 16384),
                                                 ("Wv", BIG3, 0), ("Wo", BIG3, 8192))):
            WTW[name] = prep_wT(dr["L0_attn_" + name], D, D, pool, offc, natb,
                                evac="v" if wi % 2 == 0 else "s")
            WT[name] = (pool, offc)

        def wslice(name, i, m):
            pool, offc = WT[name]
            return pool[:, offc + i * 1024 + m * 128: offc + i * 1024 + m * 128 + 128]

        def sl(k):
            return BIG1[:, k * 512:(k + 1) * 512]

        LN0, KQ, CKP, SKP, CQP, SQP, MR, MI = 0, 8, 16, 24, 32, 40, 48, 56
        RET, ZG = 32, 40

        def proj(wname, rhs_slot_fn, out_cb, bank0, extra_w=None):
            for m in range(8):
                b = (m % 4)
                tw = P.bank_wait(b)
                for i in range(8):
                    P.emit("t", "matmul", PSN[b][:, 0:512], wslice(wname, i, m),
                           rhs_slot_fn(i), start=(i == 0), stop=(i == 7),
                           waits=(tw if i == 0 else []) +
                                 (extra_w if (m == 0 and i == 0 and extra_w) else []),
                           inc="ts" if i == 7 else None)
                out_cb(m, PSN[b], b, P.w("ts"))

        RB2F = BIG1.bitcast(F32)
        L0A_BANKS = ((5, PSN[5]), ("PT", PTF0), (4, PSN[4]))

        def SAP(t0):
            return bass.AP(src_dram, t0, [[2048, 128], [128 * 2048, 8], [1, 512]])

        def RAP(t0):
            return bass.AP(dst_dram, t0, [[2048, 128], [128 * 2048, 8], [1, 512]])

        prev_fstore = None
        for tt in range(4):
            ts0 = tt * 512
            if tt < 3:
                dii, mmi = ln_prefetch(src_dram, (tt + 1) * 512, (tt + 1) % 2,
                                       entry=entry_store)
                dii()

            def ev_q(m, pb, bank, tsw):
                P.emit("s", "activation", sl(KQ + m), pb[:, 0:512], AF.Identity,
                       bias=pv("bq", m),
                       waits=[tsw] + ([kq_g_free] if (m == 0 and kq_g_free)
                                      else []), inc="ss")
                P.emit("s", "drain")
                P.bank_used(bank, P.w("ss"))
            proj("Wq", lambda i: sl(LN0 + i), ev_q, 0,
                 extra_w=[dstw] + ([WTW["Wq"]] if tt == 0 else []))
            q_done = P.w("ss")

            def phase_pass(wname, bias_name, cslot0, sslot0, rhs0, first_wait=None,
                           trig_wait=None):
                def Aslot(m):
                    return SCRF[:, (m % 2) * 512:(m % 2) * 512 + 512]

                def Tp(m):
                    return PSN[4 + m % 2][:, 0:512]

                def Bslot(m):
                    return BIG3[:, 18432 + (m % 2) * 512: 18432 + (m % 2) * 512 + 512]

                vchain = {}

                def emit_tanh(m, b, tsw):
                    ws = [tsw] + ([vchain[m - 2]] if m >= 2 else [])
                    P.emit("s", "activation", Aslot(m), PSN[b][:, 0:512], AF.Tanh,
                           bias=pv(bias_name, m), waits=ws, inc="ss")
                    P.emit("s", "drain")
                    P.bank_used(b, P.w("ss"))
                    return P.w("ss")

                trig_w = {}

                def emit_trig(m):
                    P.emit("s", "activation", sl(sslot0 + m), Aslot(m), AF.Sin,
                           scale=TWO_PI,
                           waits=[vchain[m]] +
                                 ([trig_wait] if (m == 0 and trig_wait) else []))
                    P.emit("s", "drain")
                    P.emit("s", "activation", sl(cslot0 + m), Tp(m), AF.Sin,
                           scale=TWO_PI, inc="ss")
                    P.emit("s", "drain")
                    P.bank_used(4 + m % 2, P.w("ss"))
                    trig_w[m] = P.w("ss")

                def emit_vchain(m, thw):
                    A = Aslot(m)
                    T = Tp(m)
                    Bb = Bslot(m)
                    tw = P.bank_wait(4 + m % 2)
                    P.emit("v", "tensor_scalar_mul", A, A, pv("ps2", m),
                           waits=[thw] + tw)
                    P.emit("v", "drain")
                    P.emit("v", "tensor_scalar_add", T, A, RN_C)
                    P.emit("v", "drain")
                    P.emit("v", "tensor_scalar_add", Bb, T, -RN_C)
                    P.emit("v", "drain")
                    P.emit("v", "tensor_sub", A, A, Bb)
                    P.emit("v", "drain")
                    P.emit("v", "tensor_scalar_add", T, A, 0.25)
                    P.emit("v", "drain")
                    P.emit("v", "tensor_scalar", Bb, T, 0.5, None, ALU.is_ge)
                    P.emit("v", "drain")
                    P.emit("v", "tensor_sub", T, T, Bb)
                    P.emit("v", "drain")
                    P.emit("v", "engine_nop", inc="vs")
                    vchain[m] = P.w("vs")

                for m in range(8):
                    b = m % 4
                    tw = P.bank_wait(b)
                    for i in range(8):
                        P.emit("t", "matmul", PSN[b][:, 0:512], wslice(wname, i, m),
                               sl(rhs0 + i), start=(i == 0), stop=(i == 7),
                               waits=(tw if i == 0 else []) +
                                     ((first_wait if isinstance(first_wait, list)
                                       else [first_wait])
                                      if (m == 0 and i == 0 and first_wait)
                                      else []),
                               inc="ts" if i == 7 else None)
                    thw = emit_tanh(m, b, P.w("ts"))
                    if m >= 1:
                        emit_trig(m - 1)
                    emit_vchain(m, thw)
                emit_trig(7)
                phase_pass.trig_w = trig_w
                return P.w("ss")

            qp_done = phase_pass("Wqp", "bqp", CQP, SQP, KQ,
                                 first_wait=[q_done] +
                                 ([WTW["Wqp"]] if tt == 0 else []))
            if tt < 3:
                pf[tt + 1] = mmi(row_free=[ln_apply.last_bc])
            def ev_k(m, pb, bank, tsw):
                P.emit("s", "activation", sl(KQ + m), pb[:, 0:512], AF.Identity,
                       bias=pv("bk", m), waits=[tsw, qp_done] if m == 0 else [tsw],
                       inc="ss")
                P.emit("s", "drain")
                P.bank_used(bank, P.w("ss"))
            proj("Wk", lambda i: sl(LN0 + i), ev_k, 0,
                 extra_w=[WTW["Wk"]] if tt == 0 else None)
            key_done = P.w("ss")
            kp_done = phase_pass("Wkp", "bkp", CKP, SKP, KQ,
                                 first_wait=[key_done] +
                                 ([WTW["Wkp"]] if tt == 0 else []),
                                 trig_wait=prev_fstore)
            kp_trig = dict(phase_pass.trig_w)

            def ev_v(m, pb, bank, tsw):
                P.emit("s", "activation", sl(KQ + m), pb[:, 0:512], AF.Identity,
                       bias=pv("bv", m), waits=[tsw], inc="ss")
                P.emit("s", "drain")
                P.bank_used(bank, P.w("ss"))
                sv = P.w("ss")
                P.emit("g", "tensor_mul", sl(MR + m), sl(KQ + m), sl(CKP + m),
                       waits=[sv, kp_trig[m]])
                P.emit("g", "tensor_mul", sl(MI + m), sl(KQ + m), sl(SKP + m),
                       inc="gs")
                gw = P.w("gs")
                P.emit("v", "tensor_tensor_scan", sl(MR + m), CS["scanm"][:, :],
                       sl(MR + m), 0.0, ALU.mult, ALU.add, waits=[gw])
                P.emit("v", "drain")
                P.emit("v", "tensor_tensor_scan", sl(MI + m), CS["scanm"][:, :],
                       sl(MI + m), 0.0, ALU.mult, ALU.add)
                P.emit("v", "drain")
                P.emit("v", "tensor_mul", sl(MR + m), sl(MR + m), sl(CQP + m))
                P.emit("v", "drain")
                P.emit("v", "tensor_mul", sl(MI + m), sl(MI + m), sl(SQP + m))
                P.emit("v", "drain")
                P.emit("v", "tensor_add", sl(RET + m), sl(MR + m), sl(MI + m),
                       inc="vs")
                P.emit("v", "drain")
            vs_base = P.c["vs"]
            proj("Wv", lambda i: sl(LN0 + i), ev_v, 0,
                 extra_w=[WTW["Wv"]] if tt == 0 else None)
            ret_done = P.w("vs")
            kq_g_free = P.w("gs")
            lnx_free = P.w("ts")

            dzw, _ = layer_norm(lambda i: sl(RET + i), lambda i: sl(ZG + i),
                                None, 1e-5 * 1024.0, src_is_bf=True,
                                gb=(lambda i: pv("lng", i), lambda i: pv("lnb", i)),
                                src_wait=lambda i: ("vs", vs_base + i + 1),
                                rowbase=(tt % 2) * 1024)

            wRB = hdma(RB2F[:, 4096:8192], RAP(ts0), waits=[es_w, ret_done],
                       sem="rio")
            if tt < 3:
                wR2 = hdma(RS[:, :], SAP(ts0 + 512), waits=[radd_w, es_w])
                dstw, radd_w = ln_apply((tt + 1) % 2, lambda i: Rsl(i),
                                        lambda i: sl(LN0 + i), pf[tt + 1],
                                        src_wait=[wR2, lnx_free],
                                        add_to=lambda i: Rsl(i),
                                        banks=L0A_BANKS)
                es_w = hdma(RAP(ts0 + 512), RS[:, :], waits=[radd_w],
                            sem="hes")

            def RB2l(m):
                return RB2F[:, 4096 + m * 512: 4096 + (m + 1) * 512]

            def ev_o(m, pb, bank, tsw, wRB=wRB):
                P.emit("v", "scalar_tensor_tensor", RB2l(m), pb[:, 0:512],
                       pv("bo", m), RB2l(m), ALU.add, ALU.add,
                       waits=[tsw] + ([wRB] if m == 0 else []), inc="vs")
                P.emit("v", "drain")
                P.bank_used(bank, P.w("vs"))
            proj("Wo", lambda i: sl(ZG + i), ev_o, 0,
                 extra_w=[dzw] + ([WTW["Wo"]] if tt == 0 else []))
            wout_done = P.w("vs")
            prev_fstore = hdma(RAP(ts0), RB2F[:, 4096:8192],
                               waits=[wout_done], sem="hfs")
        P.barrier()
        return prev_fstore

    def block_l0_ffn(src_dram, dst_dram, entry_store=None):
        def lnx(i, ts0=None, w=2048):
            if ts0 is None:
                return BIG1[:, i * 2048: i * 2048 + 2048]
            return BIG1[:, i * 2048 + ts0: i * 2048 + ts0 + w]

        def ysl(i, ts0):
            return BIG1[:, 16384 + i * 2048 + ts0: 16384 + i * 2048 + ts0 + 512]

        def hmsl(m):
            return BIG3[:, m * 512:(m + 1) * 512]

        BC = BIG3[:, 8192:8704]
        GMT = BIG3[:, 8704:8832]
        GMR = BIG3[:, 8832:10880]

        b = 5
        tw = P.bank_wait(b)
        for i in range(8):
            P.emit("t", "matmul", PSN[b][0:1, 0:8], CS["colf"][:, :],
                   WR32[:, i * 8:(i + 1) * 8], start=(i == 0), stop=(i == 7),
                   waits=tw if i == 0 else [], inc="ts" if i == 7 else None)
        P.emit("v", "tensor_scalar_mul", STAT[0:1, 1024:1032], PSN[b][0:1, 0:8],
               -1.0, wait=P.w("ts"), inc="vs")
        P.emit("v", "drain")
        P.emit("t", "matmul", PSN[b][:, 0:8], CS["onesall"][0:1, :],
               STAT[0:1, 1024:1032], start=True, stop=True, wait=P.w("vs"), inc="ts")
        P.emit("v", "tensor_copy", GB8[:, 0:8], PSN[b][:, 0:8], wait=P.w("ts"),
               inc="vs")
        P.emit("v", "drain")
        P.emit("t", "matmul", PSN[b][:, 0:8], CS["onesall"][0:1, :], BRROW[:, :],
               start=True, stop=True, wait=P.w("vs"), inc="ts")
        P.emit("v", "tensor_copy", GB8[:, 8:16], PSN[b][:, 0:8], wait=P.w("ts"),
               inc="vs")
        P.emit("v", "drain")
        P.bank_used(b, P.w("vs"))

        prev_store = entry_store
        lnx_done = []
        rs_free = [prev_store] if prev_store else []
        for tt in range(4):
            ts0 = tt * 512
            wR = hdma(RS[:, :],
                      bass.AP(src_dram, ts0, [[2048, 128], [128 * 2048, 8], [1, 512]]),
                      waits=rs_free or None)
            dw, _ = layer_norm(lambda i: Rsl(i), lambda i, ts0=ts0: lnx(i, ts0, 512),
                               None, 1e-5, fp32_stats=True, src_wait=wR)
            lnx_done.append(dw)
            MU = STAT[0:1, 0:512]
            RSr = STAT[0:1, 512:1024]
            MRS = STAT[0:1, 1024:1536]
            P.emit("v", "tensor_tensor", MRS, MU, RSr, ALU.mult, waits=[dw], inc="vs")
            P.emit("v", "drain")
            rows_w = P.w("vs")
            for tc in range(4):
                c0 = tc * 128
                b5 = 5
                tw5 = P.bank_wait(b5)
                for i in range(8):
                    P.emit("t", "matmul", PSN[b5][:, 0:8],
                           Rsl(i)[:, c0:c0 + 128], WR32[:, i * 8:(i + 1) * 8],
                           start=(i == 0), stop=(i == 7),
                           waits=(tw5 if i == 0 else []) + ([rows_w] if i == 0 else []),
                           inc="ts" if i == 7 else None)
                raw_w = P.w("ts")
                b6 = 2
                tw6 = P.bank_wait(b6)
                P.emit("t", "transpose", PSN[b6][:, 0:1], RSr[0:1, c0:c0 + 128],
                       CS["idf"][0:1, 0:1], waits=tw6)
                P.emit("t", "transpose", PSN[b6][:, 1:2], MRS[0:1, c0:c0 + 128],
                       CS["idf"][0:1, 0:1], inc="ts")
                P.emit("v", "tensor_copy", SCRF[:, 0:2], PSN[b6][:, 0:2],
                       wait=P.w("ts"), inc="vs")
                P.emit("v", "drain")
                P.bank_used(b6, P.w("vs"))
                rs_c = SCRF[:, 0:1]
                mrs_c = SCRF[:, 1:2]
                gl = SCRF[:, 8:16]
                P.emit("v", "tensor_scalar", gl, PSN[b5][:, 0:8], rs_c, None,
                       ALU.mult, waits=[raw_w], inc="vs")
                P.emit("v", "drain")
                P.bank_used(b5, P.w("vs"))
                P.emit("v", "scalar_tensor_tensor", gl, GB8[:, 0:8], mrs_c, gl,
                       ALU.mult, ALU.add)
                P.emit("v", "drain")
                P.emit("v", "tensor_add", gl, gl, GB8[:, 8:16])
                P.emit("v", "drain")
                mx = SCRF[:, 4:5]
                nmx = SCRF[:, 5:6]
                ssum = SCRF[:, 2:3]
                ginv = SCRF[:, 3:4]
                mn = SCRF[:, 6:7]
                et = SCRF[:, 16:24]
                msk = SCRF[:, 24:32]
                P.emit("v", "reduce_max", mx, gl, AX.X)
                P.emit("v", "drain")
                P.emit("v", "tensor_scalar_mul", nmx, mx, -1.0)
                P.emit("v", "drain")
                P.emit("v", "engine_nop", inc="vs")
                P.emit("s", "activation", et, gl, AF.Exp, bias=nmx,
                       accum_out=ssum, wait=P.w("vs"), inc="ss")
                P.emit("s", "drain")
                P.emit("v", "reciprocal", ginv, ssum, wait=P.w("ss"))
                P.emit("v", "drain")
                P.emit("v", "tensor_scalar", msk, gl, mx, None, ALU.is_ge)
                P.emit("v", "drain")
                P.emit("v", "tensor_scalar", msk, msk, -1000.0, 1000.0,
                       ALU.mult, ALU.add)
                P.emit("v", "drain")
                P.emit("v", "tensor_add", msk, msk, CS["iota8"][:, :])
                P.emit("v", "drain")
                P.emit("v", "tensor_reduce", mn, msk, AX.X, ALU.min)
                P.emit("v", "drain")
                P.emit("v", "tensor_scalar", msk, msk, mn, None, ALU.is_equal)
                P.emit("v", "drain")
                P.emit("v", "tensor_scalar", GMT[:, (tt * 4 + tc) * 8:
                                                 (tt * 4 + tc) * 8 + 8],
                       msk, ginv, None, ALU.mult, inc="vs")
                P.emit("v", "drain")
            rs_free = [P.w("ts"), P.w("vs")]
        gmt_done = P.w("vs")
        lnx_all = lnx_done[-1]

        wa = None
        wb = None
        h_end = {}
        w2_end = {}
        ye7 = {}
        b2w = {}
        for e in range(8):
            gw1 = [h_end[e - 1]] if e >= 1 else None
            for i in range(8):
                wa = gload(BIG2[:, i * 2048:(i + 1) * 2048],
                           dr["L0_ffn_W1"][e * 1024 + i * 128: e * 1024 + (i + 1) * 128, :],
                           sem="wa0", waits=gw1 if i == 0 else None)
            gw2 = [w2_end[e - 1]] if e >= 1 else None
            for h in range(16):
                wb = gload(BIG2[:, 16384 + h * 1024: 16384 + (h + 1) * 1024],
                           dr["L0_ffn_W2"][e * 2048 + h * 128: e * 2048 + (h + 1) * 128, :],
                           sem="wb0", waits=gw2 if h == 0 else None)
            b2w[e] = gload(B2E[0:1, :], dr["L0_ffn_b2"][e:e + 1, :], sem="wb1",
                           waits=gw2)
            for g0 in range(2):
                tpt = P.bank_wait("PT")
                for j in range(8):
                    tc = g0 * 8 + j
                    wts = (tpt if j == 0 else []) + \
                          ([gmt_done] if (g0 == 0 and j == 0 and e == 0) else [])
                    P.emit("t", "transpose", PT[0:1, j * 128:(j + 1) * 128],
                           GMT[:, tc * 8 + e: tc * 8 + e + 1], CS["idb"][:, :],
                           waits=wts, inc="ts")
                P.emit("v", "tensor_copy", GMR[0:1, g0 * 1024:(g0 + 1) * 1024],
                       PT[0:1, 0:1024], wait=P.w("ts"), inc="vs")
                P.emit("v", "drain")
                P.bank_used("PT", P.w("vs"))
            gmr_w = P.w("vs")

            for tt in range(4):
                ts0 = tt * 512
                b6 = 2
                tw6 = P.bank_wait(b6)
                P.emit("t", "matmul", PSN[b6][:, 0:512], CS["ones1b"][:, :],
                       GMR[0:1, ts0:ts0 + 512], start=True, stop=True,
                       waits=tw6 + [gmr_w], inc="ts")
                P.emit("v", "tensor_copy", BC[:, :], PSN[b6][:, 0:512],
                       wait=P.w("ts"), inc="vs")
                P.emit("v", "drain")
                P.bank_used(b6, P.w("vs"))
                bc_w = P.w("vs")
                for m in range(16):
                    bk = m % 4
                    twk = P.bank_wait(bk)
                    for i in range(8):
                        ws = (twk if i == 0 else [])
                        if m == 0 and i == 0 and tt == 0:
                            ws = ws + [wa, lnx_all]
                        P.emit("t", "matmul", PSN[bk][:, 0:512],
                               BIG2[:, i * 2048 + m * 128: i * 2048 + m * 128 + 128],
                               lnx(i, ts0, 512), start=(i == 0), stop=(i == 7),
                               waits=ws, inc="ts" if i == 7 else None)
                    tsw = P.w("ts")
                    P.emit("s", "activation", hmsl(m), PSN[bk][:, 0:512], AF.Relu,
                           bias=B1C[:, e * 16 + m: e * 16 + m + 1],
                           waits=[tsw], inc="ss")
                    P.emit("s", "drain")
                    P.emit("v", "tensor_mul", hmsl(m), hmsl(m), BC[:, :],
                           waits=[P.w("ss"), bc_w] if m == 0 else [P.w("ss")],
                           inc="vs")
                    P.emit("v", "drain")
                    P.bank_used(bk, P.w("vs"))
                hm_w = P.w("vs")
                if tt == 3:
                    h_end[e] = P.w("ts")
                for dd in range(8):
                    bk = dd % 4
                    twk = P.bank_wait(bk)
                    for h in range(16):
                        ws = (twk if h == 0 else [])
                        if dd == 0 and h == 0:
                            ws = ws + [hm_w]
                            if tt == 0:
                                ws = ws + [wb, b2w[e]]
                        P.emit("t", "matmul", PSN[bk][:, 0:512],
                               BIG2[:, 16384 + h * 1024 + dd * 128:
                                    16384 + h * 1024 + dd * 128 + 128],
                               hmsl(h), start=(h == 0), stop=False,
                               waits=ws)
                    P.emit("t", "matmul", PSN[bk][:, 0:512],
                           B2E[0:1, dd * 128:(dd + 1) * 128],
                           GMR[0:1, ts0:ts0 + 512], start=False, stop=True,
                           inc="ts")
                    tsw = P.w("ts")
                    if e == 0:
                        P.emit("v", "tensor_copy", ysl(dd, ts0), PSN[bk][:, 0:512],
                               waits=[tsw], inc="vs")
                    else:
                        P.emit("v", "tensor_tensor", ysl(dd, ts0), PSN[bk][:, 0:512],
                               ysl(dd, ts0), ALU.add, waits=[tsw], inc="vs")
                    P.emit("v", "drain")
                    P.bank_used(bk, P.w("vs"))
                if tt == 3:
                    w2_end[e] = P.w("ts")
                if e == 7:
                    ye7[tt] = P.w("vs")
        y_done = P.w("vs")

        rsfree_d = rs_free
        for tt in range(4):
            ts0 = tt * 512
            wR = hdma(RS[:, :],
                      bass.AP(src_dram, ts0, [[2048, 128], [128 * 2048, 8], [1, 512]]),
                      waits=rsfree_d if tt == 0 else [prev_store])
            for i in range(8):
                P.emit("v", "tensor_tensor", Rsl(i), ysl(i, ts0), Rsl(i), ALU.add,
                       waits=[wR, ye7[tt]] if i == 0 else [],
                       inc="vs" if i == 7 else None)
                P.emit("v", "drain")
            prev_store = hdma(bass.AP(dst_dram, ts0,
                                      [[2048, 128], [128 * 2048, 8], [1, 512]]),
                              RS[:, :], waits=[P.w("vs")], sem="hfs")
        for wi, wname in enumerate(("L1_attn_w_down_q", "L1_attn_w_down_kv")):
            for r in range(4):
                c0 = 10880 + (wi * 4 + r) * 1024
                gload(BIG3[:, c0:c0 + 1024], dr[wname][r * 128:(r + 1) * 128, :],
                      sem="wpre")
        PRELD["dn"] = P.w("wpre")
        P.barrier()
        return prev_store

    def block_l1_attn(src_dram, dst_dram, entry_store=None):
        def kres(i):
            return BIG1[:, i * 2048:(i + 1) * 2048]

        def qres(i):
            return BIG1[:, 16384 + i * 2048: 16384 + (i + 1) * 2048]

        def outsb(i):
            return BIG2[:, 16640 + i * 512: 16640 + (i + 1) * 512]

        def pslot(i):
            return BIG2[:, 20736 + i * 512: 20736 + (i + 1) * 512]

        def hsl(i):
            return BIG2[:, 22784 + i * 512: 22784 + (i + 1) * 512]

        def cq(i):
            return BIG2[:, 26880 + i * 512: 26880 + (i + 1) * 512]

        def ckv(i):
            return BIG2[:, 28928 + i * 512: 28928 + (i + 1) * 512]

        RSC1 = BIG2[:, 30976:31488]
        RSC2 = BIG2[:, 31488:32000]
        b2t = BIG2.tensor if hasattr(BIG2, "tensor") else BIG2

        natb = [BIG2[:, 0:1024], BIG2[:, 1024:2048]]
        if "dn" in PRELD:
            pq = (lambda r: BIG3[:, 10880 + r * 1024: 10880 + (r + 1) * 1024],
                  PRELD["dn"])
            pk = (lambda r: BIG3[:, 14976 + r * 1024: 14976 + (r + 1) * 1024],
                  PRELD["dn"])
            prep_wT(dr["L1_attn_w_down_q"], 512, D, BIG3, 0, None, pre=pq)
            prep_wT(dr["L1_attn_w_down_kv"], 512, D, BIG3, 4096, None, evac="s",
                    pre=pk)
        else:
            prep_wT(dr["L1_attn_w_down_q"], 512, D, BIG3, 0, natb)
            prep_wT(dr["L1_attn_w_down_kv"], 512, D, BIG3, 4096, natb, evac="s")
        prep_wT(dr["L1_attn_w_up_q"], D, 512, BIG3, 8192, natb)
        prep_wT(dr["L1_attn_w_up_k"], D, 512, BIG3, 12288, natb, evac="s")
        prep_wT(dr["L1_attn_w_up_v"], D, 512, BIG3, 16384, natb)
        prep_done = [P.w("vs"), P.w("ss")]
        P.emit("t", None, waits=prep_done)
        P.emit("v", "memset",
               bass.AP(b2t, 64, [[32768, 128], [1040, 16], [65, 16]]), 1.0,
               waits=prep_done, inc="msn", incn=16)
        vone_w = P.w("msn")

        def wdq(i, mc):
            return BIG3[:, i * 512 + mc * 128: i * 512 + mc * 128 + 128]

        def wdkv(i, mc):
            return BIG3[:, 4096 + i * 512 + mc * 128: 4096 + i * 512 + mc * 128 + 128]

        def wup(base, c, m, w=128):
            return BIG3[:, base + c * 1024 + m * w: base + c * 1024 + (m + 1) * w]

        QM = QKM[0:16, 0:1]
        KM = QKM[0:16, 1:2]
        RS2R = STAT[0:1, 2048:2560]
        pfA = {}
        diA, mmA = ln_prefetch(src_dram, 0, 0, rs2_row=RS2R, entry=entry_store)
        diA()
        pfA[0] = mmA()

        def SAP(t0):
            return bass.AP(src_dram, t0, [[2048, 128], [128 * 2048, 8], [1, 512]])

        def DAP(t0):
            return bass.AP(dst_dram, t0, [[2048, 128], [128 * 2048, 8], [1, 512]])

        def apply_l1(tt, wRx, extra_sw=None):
            return ln_apply(tt % 2, lambda i: Rsl(i), None, pfA[tt],
                            src_wait=[wRx] + (extra_sw or []),
                            add_to=lambda i: Rsl(i),
                            rs2_row=RS2R, dst2=lambda i: hsl(i),
                            gb2=(lambda i: pv("l1s", i), lambda i: pv("l1b", i)),
                            banks=((4, PSN[4]), (3, PSN[3]), ("PT", PTF0)))

        wR = hdma(RS[:, :], SAP(0), waits=[entry_store] if entry_store else None)
        dhw, dw = apply_l1(0, wR)
        prev_store = hdma(DAP(0), RS[:, :], waits=[dw], sem="hes")
        rope_used = None
        phaseA_last_dp = None
        for tt in range(4):
            ts0 = tt * 512
            wr2a = gload(ROPE2[:, 0:512], dr["ropeC"][:, ts0:ts0 + 512], sem="wr2",
                         waits=list(rope_used) if rope_used else None)
            wr2 = gload(ROPE2[:, 512:1024], dr["ropeS"][:, ts0:ts0 + 512], sem="wr2")
            if tt < 3:
                diA2, mmA2 = ln_prefetch(src_dram, (tt + 1) * 512, (tt + 1) % 2,
                                         rs2_row=RS2R, entry=entry_store)
                diA2()
            def dp(wfn, dst, extra=None):
                for mc in range(4):
                    bk = mc % 2
                    tw = P.bank_wait(bk)
                    for i in range(8):
                        ws = (tw if i == 0 else [])
                        if mc == 0 and i == 0:
                            ws = ws + [dhw] + (extra or [])
                        P.emit("t", "matmul", PSN[bk][:, 0:512], wfn(i, mc),
                               hsl(i), start=(i == 0), stop=(i == 7),
                               waits=ws, inc="ts" if i == 7 else None)
                    P.emit("s", "activation", dst(mc), PSN[bk][:, 0:512],
                           AF.Identity, waits=[P.w("ts")], inc="ss")
                    P.emit("s", "drain")
                    P.bank_used(bk, P.w("ss"))
            dp(wdq, cq)
            dp(wdkv, ckv)
            cqw = P.w("ss")
            phaseA_last_dp = P.w("ts")
            hsl_free = P.w("ts")
            if tt < 3:
                pfA[tt + 1] = mmA2(row_free=[ln_apply.last_bc])

            def up(base, src_fn, dst_fn, do_rope, norm_dst):
                sq_w = {}

                def emit_norm(mm):
                    tw5 = P.bank_wait(5) if mm == 0 else []
                    P.emit("t", "matmul", PSN[5][0:16, 0:512],
                           CS["e16"][:, mm * 16:(mm + 1) * 16], RSC2,
                           start=(mm == 0), stop=(mm == 7),
                           waits=[sq_w[mm]] + tw5,
                           inc="ts" if mm == 7 else None)

                for m in range(8):
                    bk = m % 2
                    tw = P.bank_wait(bk)
                    for c in range(4):
                        ws = (tw if c == 0 else [])
                        if m == 0 and c == 0:
                            ws = ws + [cqw]
                        P.emit("t", "matmul", PSN[bk][:, 0:512],
                               wup(base, c, m), src_fn(c),
                               start=(c == 0), stop=(c == 3),
                               waits=ws, inc="ts" if c == 3 else None)
                    if do_rope and m >= 1:
                        emit_norm(m - 1)
                    P.emit("s", "activation", dst_fn(m)[:, ts0:ts0 + 512],
                           PSN[bk][:, 0:512], AF.Identity,
                           waits=[P.w("ts")], inc="ss")
                    P.emit("s", "drain")
                    P.bank_used(bk, P.w("ss"))
                    evw = P.w("ss")
                    if do_rope:
                        bk2 = 2
                        tw2 = P.bank_wait(bk2)
                        P.emit("t", "matmul", PSN[bk2][:, 0:512], CS["swapb"][:, :],
                               dst_fn(m)[:, ts0:ts0 + 512], start=True, stop=True,
                               waits=tw2 + [evw] +
                                     ([wr2, vone_w] if (m == 0 and tt >= 0) else []),
                               inc="ts")
                        swp = P.w("ts")
                        P.emit("g", "tensor_mul", RSC1,
                               dst_fn(m)[:, ts0:ts0 + 512], ROPE2[:, 0:512],
                               waits=[evw] + ([wr2] if m == 0 else []), inc="gs")
                        gw = P.w("gs")
                        P.emit("v", "tensor_tensor", RSC2, PSN[bk2][:, 0:512],
                               ROPE2[:, 512:1024], ALU.mult, waits=[swp])
                        P.emit("v", "drain")
                        P.emit("v", "engine_nop", inc="vs")
                        P.bank_used(bk2, P.w("vs"))
                        P.emit("v", "tensor_add", dst_fn(m)[:, ts0:ts0 + 512],
                               RSC1, RSC2, waits=[gw], inc="vs")
                        P.emit("v", "drain")
                        rw = P.w("vs")
                        P.emit("s", "activation", RSC2, dst_fn(m)[:, ts0:ts0 + 512],
                               AF.Square, waits=[rw], inc="ss")
                        P.emit("s", "drain")
                        sq_w[m] = P.w("ss")
                if do_rope:
                    emit_norm(7)
                nw = P.w("ts")
                P.emit("v", "reduce_max", QKM[0:16, 2:3], PSN[5][0:16, 0:512],
                       AX.X, waits=[nw])
                P.emit("v", "drain")
                if tt == 0:
                    P.emit("v", "tensor_copy", norm_dst, QKM[0:16, 2:3], inc="vs")
                else:
                    P.emit("v", "tensor_max", norm_dst, norm_dst,
                           QKM[0:16, 2:3], inc="vs")
                P.emit("v", "drain")
                P.bank_used(5, P.w("vs"))

            up(8192, cq, qres, True, QM)
            up(12288, ckv, kres, True, KM)
            rope_used = [P.w("gs"), P.w("vs")]

            if tt < 3:
                wR2 = hdma(RS[:, :], SAP(ts0 + 512), waits=[dw, prev_store])
                dhw, dw = apply_l1(tt + 1, wR2, extra_sw=[hsl_free])
                prev_store = hdma(DAP(ts0 + 512), RS[:, :], waits=[dw],
                                  sem="hes")

            for c in range(4):
                tc = tt * 4 + c
                for oh in range(2):
                    bk = oh % 2
                    tw = P.bank_wait(bk)
                    for ic in range(4):
                        ws = (tw if ic == 0 else [])
                        P.emit("t", "matmul", PSN[bk][:, 0:512],
                               ckv(ic)[:, c * 128:(c + 1) * 128],
                               BIG3[:, 16384 + ic * 1024 + oh * 512:
                                    16384 + ic * 1024 + (oh + 1) * 512],
                               start=(ic == 0), stop=(ic == 3),
                               waits=ws, inc="ts" if ic == 3 else None)
                    P.emit("v", "tensor_copy",
                           bass.AP(b2t, tc * 1040 + oh * 520,
                                   [[32768, 128], [65, 8], [1, 64]]),
                           PSN[bk][:, 0:512], waits=[P.w("ts")], inc="vs")
                    P.emit("v", "drain")
                    P.bank_used(bk, P.w("vs"))
        vaug_w = P.w("vs")

        natb2 = [BIG3[:, 0:1024], BIG3[:, 1024:2048]]
        P.emit("g", None, wait=phaseA_last_dp)
        wout_prep = prep_wT(dr["L1_attn_w_out"], D, D, BIG2, 22784, natb2)

        P.emit("v", "tensor_tensor", QKM[0:16, 3:4], QM, KM, ALU.mult, inc="vs")
        P.emit("v", "drain")
        P.emit("s", "activation", QKM[0:16, 3:4], QKM[0:16, 3:4], AF.Sqrt,
               wait=P.w("vs"), inc="ss")
        P.emit("s", "drain")
        P.emit("v", "tensor_scalar_mul", QKM[0:16, 3:4], QKM[0:16, 3:4],
               -1.05 / 8.0, wait=P.w("ss"), inc="vs")
        P.emit("v", "drain")
        bk6 = 2
        tw6 = P.bank_wait(bk6)
        P.emit("t", "transpose", PSN[bk6][0:1, 0:16], QKM[0:16, 3:4],
               CS["idf"][0:16, 0:16], waits=tw6 + [P.w("vs")], inc="ts")
        P.emit("v", "tensor_copy", STAT[0:1, 1024:1040], PSN[bk6][0:1, 0:16],
               wait=P.w("ts"), inc="vs")
        P.emit("v", "drain")
        P.bank_used(bk6, P.w("vs"))
        for h in range(16):
            tw6 = P.bank_wait(bk6)
            P.emit("t", "matmul", PSN[bk6][:, 0:1], CS["onesall"][0:1, :],
                   STAT[0:1, 1024 + h:1025 + h], start=True, stop=True,
                   waits=tw6 + ([P.w("vs")] if h == 0 else []), inc="ts")
            P.emit("v", "tensor_copy", SCRF[:, h:h + 1], PSN[bk6][:, 0:1],
                   wait=P.w("ts"), inc="vs")
            P.emit("v", "drain")
            P.bank_used(bk6, P.w("vs"))
        bias_w = P.w("vs")

        LIN = SCRF[64:65, 512:1024]
        LBC = SCRF[0:64, 512:1024]
        PTF = PT.bitcast(F32)
        for tq in range(4):
            ts0 = tq * 512

            LIN2 = STAT[64:65, 0:512]

            def norm_pair(hp, av_done):
                hA, hB = 2 * hp, 2 * hp + 1
                P.emit("v", "reciprocal", LIN, PSN[4][64:65, 0:512],
                       waits=[av_done])
                P.emit("v", "drain")
                P.emit("v", "reciprocal", LIN2, PSN[5][64:65, 0:512], inc="vs")
                P.emit("v", "drain")
                lw = P.w("vs")
                for oi, (hh, ob, lrow) in enumerate(((hA, 4, LIN),
                                                    (hB, 5, LIN2))):
                    twp = P.bank_wait("PT")
                    P.emit("t", "matmul", PTF[0:64, 0:512],
                           CS["onesall"][64:65, 0:64], lrow,
                           start=True, stop=True,
                           waits=twp + ([lw] if oi == 0 else []), inc="ts")
                    P.emit("v", "tensor_copy", LBC, PTF[0:64, 0:512],
                           wait=P.w("ts"))
                    P.emit("v", "drain")
                    P.emit("v", "engine_nop", inc="vs")
                    P.bank_used("PT", P.w("vs"))
                    P.emit("v", "tensor_tensor",
                           outsb(hp)[(hh % 2) * 64:(hh % 2) * 64 + 64, :],
                           PSN[ob][0:64, 0:512], LBC, ALU.mult, inc="vs")
                    P.emit("v", "drain")
                    P.bank_used(ob, P.w("vs"))

            pend = None
            for hp in range(8):
                hA, hB = 2 * hp, 2 * hp + 1
                pw = {}
                sw = {}

                def s_pair(tk, w0=None):
                    sA, sB = (2 * tk) % 4, (2 * tk + 1) % 4
                    twA = P.bank_wait(sA)
                    twB = P.bank_wait(sB)
                    P.emit("t", "matmul", PSN[sA][:, 0:512],
                           kres(hp)[0:64, tk * 128:(tk + 1) * 128],
                           qres(hp)[0:64, ts0:ts0 + 512], start=True, stop=True,
                           tile_position=(0, 0), waits=twA + (w0 or []), inc="ts")
                    sw[(tk, 0)] = P.w("ts")
                    P.emit("t", "matmul", PSN[sB][:, 0:512],
                           kres(hp)[64:128, tk * 128:(tk + 1) * 128],
                           qres(hp)[64:128, ts0:ts0 + 512], start=True, stop=True,
                           tile_position=(64, 0), waits=twB, inc="ts")
                    sw[(tk, 1)] = P.w("ts")

                w0 = [bias_w, vaug_w, wout_prep] if (tq == 0 and hp == 0) else None
                s_pair(0, w0)
                s_pair(1)
                if pend is not None:
                    norm_pair(*pend)
                    pend = None
                for tk in range(16):
                    if 1 <= tk < 15:
                        s_pair(tk + 1)
                    sA, sB = (2 * tk) % 4, (2 * tk + 1) % 4
                    pA, pB = tk % 2, 2 + tk % 2
                    P.emit("s", "activation", pslot(pA), PSN[sA][:, 0:512], AF.Exp,
                           scale=0.125, bias=SCRF[:, hA:hA + 1],
                           waits=[sw[(tk, 0)]] + ([pw[tk - 2]] if tk >= 2 else []),
                           inc="ss")
                    P.bank_used(sA, P.w("ss"))
                    eAw = P.w("ss")
                    P.emit("s", "activation", pslot(pB), PSN[sB][:, 0:512], AF.Exp,
                           scale=0.125, bias=SCRF[:, hB:hB + 1],
                           waits=[sw[(tk, 1)]], inc="ss")
                    P.bank_used(sB, P.w("ss"))
                    eBw = P.w("ss")
                    avw = P.bank_wait(4) if tk == 0 else []
                    P.emit("t", "matmul", PSN[4][0:65, 0:512],
                           bass.AP(b2t, tk * 1040 + hA * 65, [[32768, 128], [1, 65]]),
                           pslot(pA), start=(tk == 0), stop=(tk == 15),
                           waits=[eAw] + avw, inc="ts")
                    avw = P.bank_wait(5) if tk == 0 else []
                    P.emit("t", "matmul", PSN[5][0:65, 0:512],
                           bass.AP(b2t, tk * 1040 + hB * 65, [[32768, 128], [1, 65]]),
                           pslot(pB), start=(tk == 0), stop=(tk == 15),
                           waits=[eBw] + avw, inc="ts")
                    pw[tk] = P.w("ts")
                pend = (hp, P.w("ts"))
            norm_pair(*pend)
            osb_w = P.w("vs")
            wR2 = hdma(RS[:, :],
                       bass.AP(dst_dram, ts0, [[2048, 128], [128 * 2048, 8], [1, 512]]),
                       waits=[prev_store] if tq == 0 else [prev_store])
            for m in range(8):
                bk = m % 2
                tw = P.bank_wait(bk)
                for i in range(8):
                    ws = (tw if i == 0 else [])
                    if m == 0 and i == 0:
                        ws = ws + [osb_w]
                    P.emit("t", "matmul", PSN[bk][:, 0:512],
                           BIG2[:, 22784 + i * 1024 + m * 128:
                                22784 + i * 1024 + m * 128 + 128],
                           outsb(i), start=(i == 0), stop=(i == 7),
                           waits=ws, inc="ts" if i == 7 else None)
                P.emit("v", "tensor_tensor", Rsl(m), PSN[bk][:, 0:512], Rsl(m),
                       ALU.add, waits=[P.w("ts")] + ([wR2] if m == 0 else []),
                       inc="vs")
                P.emit("v", "drain")
                P.bank_used(bk, P.w("vs"))
            prev_store = hdma(bass.AP(dst_dram, ts0,
                                      [[2048, 128], [128 * 2048, 8], [1, 512]]),
                              RS[:, :], waits=[P.w("vs")], sem="hfs")
        P.barrier()
        return prev_store

    def block_l1_ffn(src_dram, dst_dram, entry_store=None):
        def usl(m, th=None):
            b = m * 1024
            if th is None:
                return BIG1[:, b:b + 1024]
            return BIG1[:, b + th * 512: b + th * 512 + 512]

        def g1sl(m, th=None):
            b = m * 1024
            if th is None:
                return BIG2[:, b:b + 1024]
            return BIG2[:, b + th * 512: b + th * 512 + 512]

        def lnx4(i, th):
            return BIG3[:, i * 1024 + th * 512: i * 1024 + th * 512 + 512]

        NA = [BIG3[:, 8192:9216], BIG3[:, 9216:10240]]
        NB = [BIG3[:, 10240:11264], BIG3[:, 11264:12288]]
        NH = [BIG3[:, 8192:10240], BIG3[:, 10240:12288]]
        TRS = [12288, 16384]

        prev_store = entry_store
        rs_free = []
        hists = {"a": [], "b": [], "h": [], "y": []}
        pfF = {}
        for th in range(2):
            diF, mmF = ln_prefetch(src_dram, th * 512, th, entry=entry_store,
                                   banks=((4, PSN[4]), (5, PSN[5])))
            diF()
            pfF[(0, th)] = mmF(row_free=[ln_apply.last_bc]
                               if hasattr(ln_apply, "last_bc") else None)
        for tt in range(2):
            base = tt * 1024
            lnw = []
            for th in range(2):
                ts0 = base + th * 512
                wR = hdma(RS[:, :],
                          bass.AP(src_dram, ts0,
                                  [[2048, 128], [128 * 2048, 8], [1, 512]]),
                          waits=(([prev_store] if prev_store else []) + rs_free)
                          or None)
                dw, _ = ln_apply(th, lambda i: Rsl(i),
                                 lambda i, th=th: lnx4(i, th),
                                 pfF[(tt, th)], src_wait=wR)
                lnw.append(dw)
                rs_free = [P.w("vs")]
            if tt == 0:
                diF0, mmF0 = ln_prefetch(src_dram, 1024, 0, entry=entry_store,
                                         banks=((4, PSN[4]), (5, PSN[5])))
                diF0()
            wmk = P.w("ts")
            hists["a"] += [wmk, wmk]
            hists["b"] += [wmk, wmk]
            for m in range(32):
                tro = TRS[m % 2]
                w1w = prep_wT(dr["L1_ffn_w1_w"][m * 128:(m + 1) * 128, :], 128, 1024,
                              BIG3, tro, NA, hist=hists["a"],
                              nslots=2, sem=("wa0", "wa1"), deep=True)
                w2w = prep_wT(dr["L1_ffn_w2_w"][m * 128:(m + 1) * 128, :], 128, 1024,
                              BIG3, tro + 1024, NB,
                              hist=hists["b"], nslots=2, sem=("wb0", "wb1"),
                              evac="s", deep=True)
                for th in range(2):
                    bA, bB = th % 2, 2 + th % 2
                    twA = P.bank_wait(bA)
                    twB = P.bank_wait(bB)
                    for i in range(8):
                        ws = (twA if i == 0 else [])
                        if i == 0 and th == 0:
                            ws = ws + [w1w] + ([lnw[-1]] if m == 0 else [])
                        P.emit("t", "matmul", PSN[bA][:, 0:512],
                               BIG3[:, tro + i * 128: tro + (i + 1) * 128],
                               lnx4(i, th), start=(i == 0), stop=(i == 7),
                               waits=ws, inc="ts" if i == 7 else None)
                    aw = P.w("ts")
                    for i in range(8):
                        ws = (twB if i == 0 else [])
                        if i == 0 and th == 0:
                            ws = ws + [w2w]
                        P.emit("t", "matmul", PSN[bB][:, 0:512],
                               BIG3[:, tro + 1024 + i * 128: tro + 1024 + (i + 1) * 128],
                               lnx4(i, th), start=(i == 0), stop=(i == 7),
                               waits=ws, inc="ts" if i == 7 else None)
                    bw = P.w("ts")
                    P.emit("s", "activation", g1sl(m, th), PSN[bA][:, 0:512],
                           AF.Silu, scale=scol("L1_ffn_w1_s", m),
                           waits=[aw], inc="ss")
                    P.emit("s", "drain")
                    P.bank_used(bA, P.w("ss"))
                    P.emit("v", "scalar_tensor_tensor", usl(m, th), PSN[bB][:, 0:512],
                           scol("L1_ffn_w2_s", m), g1sl(m, th), ALU.mult, ALU.mult,
                           waits=[bw, P.w("ss")], inc="vs")
                    P.emit("v", "drain")
                    P.bank_used(bB, P.w("vs"))
            u_done = P.w("vs")
            if tt == 0:
                pfF[(1, 0)] = mmF0(row_free=[ln_apply.last_bc])
                diF1, mmF1 = ln_prefetch(src_dram, 1024 + 512, 1,
                                         entry=entry_store,
                                         banks=((4, PSN[4]), (5, PSN[5])))
                diF1()
                pfF[(1, 1)] = mmF1(row_free=[ln_apply.last_bc])

            def hidden_layer(wname, sname, src_sl, dst_sl, first_w):
                wmk2 = P.w("ts")
                hists["h"] += [wmk2, wmk2]
                for m in range(32):
                    tro = TRS[m % 2]
                    hwA = prep_wT(dr[wname][m * 128:(m + 1) * 128, 0:2048],
                                  128, 2048, BIG3, tro, NH, hist=hists["h"],
                                  nslots=2, sem=("wb0", "wb1"), deep=True)
                    hwB = prep_wT(dr[wname][m * 128:(m + 1) * 128, 2048:4096],
                                  128, 2048, BIG3, tro + 2048, NH,
                                  hist=hists["h"], nslots=2, sem=("wb0", "wb1"),
                                  deep=True)
                    for th in range(2):
                        bk = th % 2
                        tw = P.bank_wait(bk)
                        for k in range(32):
                            ws = (tw if k == 0 else [])
                            if k == 0 and th == 0:
                                ws = ws + [hwA] + ([first_w] if m == 0 else [])
                            if k == 16 and th == 0:
                                ws = ws + [hwB]
                            P.emit("t", "matmul", PSN[bk][:, 0:512],
                                   BIG3[:, tro + k * 128: tro + (k + 1) * 128],
                                   src_sl(k, th), start=(k == 0), stop=(k == 31),
                                   waits=ws, inc="ts" if k == 31 else None)
                        P.emit("s", "activation", dst_sl(m, th), PSN[bk][:, 0:512],
                               AF.Gelu, scale=scol(sname, m),
                               waits=[P.w("ts")], inc="ss")
                        P.emit("s", "drain")
                        P.bank_used(bk, P.w("ss"))
                return P.w("ss")

            g1_done = hidden_layer("L1_ffn_h1_w", "L1_ffn_h1_s",
                                   usl, g1sl, u_done)
            g2_done = hidden_layer("L1_ffn_h2_w", "L1_ffn_h2_s",
                                   g1sl, usl, g1_done)

            G1F = BIG2.bitcast(F32)
            for m in range(8):
                tro = TRS[m % 2]
                w3wA = prep_wT(dr["L1_ffn_w3_w"][m * 128:(m + 1) * 128, 0:2048],
                               128, 2048, BIG3, tro, NH, hist=hists["h"],
                               nslots=2, sem=("wb0", "wb1"), deep=True)
                w3wB = prep_wT(dr["L1_ffn_w3_w"][m * 128:(m + 1) * 128,
                                                 2048:4096],
                               128, 2048, BIG3, tro + 2048, NH,
                               hist=hists["h"], nslots=2, sem=("wb0", "wb1"),
                               deep=True)
                for th in range(2):
                    bk = th % 2
                    tw = P.bank_wait(bk)
                    for k in range(32):
                        ws = (tw if k == 0 else [])
                        if k == 0 and th == 0:
                            ws = ws + [w3wA] + ([g2_done] if m == 0 else [])
                        if k == 16 and th == 0:
                            ws = ws + [w3wB]
                        P.emit("t", "matmul", PSN[bk][:, 0:512],
                               BIG3[:, tro + k * 128: tro + (k + 1) * 128],
                               usl(k, th), start=(k == 0), stop=(k == 31),
                               waits=ws, inc="ts" if k == 31 else None)
                    P.emit("s", "activation",
                           G1F[:, th * 4096 + m * 512: th * 4096 + (m + 1) * 512],
                           PSN[bk][:, 0:512], AF.Identity,
                           scale=scol("L1_ffn_w3_s", m), waits=[P.w("ts")],
                           inc="ss")
                    P.emit("s", "drain")
                    P.bank_used(bk, P.w("ss"))
            yph_w = P.w("ss")
            for th in range(2):
                ts0 = base + th * 512
                wR = hdma(RS[:, :],
                          bass.AP(src_dram, ts0,
                                  [[2048, 128], [128 * 2048, 8], [1, 512]]),
                          waits=rs_free + ([prev_store] if prev_store else []))
                for m in range(8):
                    P.emit("v", "tensor_tensor", Rsl(m),
                           G1F[:, th * 4096 + m * 512: th * 4096 + (m + 1) * 512],
                           Rsl(m), ALU.add,
                           waits=[wR, yph_w] if m == 0 else [], inc="vs")
                    P.emit("v", "drain")
                prev_store = hdma(bass.AP(dst_dram, ts0,
                                          [[2048, 128], [128 * 2048, 8], [1, 512]]),
                                  RS[:, :], waits=[P.w("vs")], sem="hfs")
                rs_free = []
        P.barrier()
        return prev_store

    BLOCK_FN = {}
    BLOCK_FN[0] = block_l0_attn
    BLOCK_FN[1] = block_l0_ffn
    BLOCK_FN[2] = block_l1_attn
    BLOCK_FN[3] = block_l1_ffn

    srcs = [dr["xT"], dr["R"], dr["R"], dr["R"]]
    dsts = [dr["R"], dr["R"], dr["R"], dr["R"]]
    dsts[NBLOCKS - 1] = dr["out"]

    last_store = None
    for bi in range(NBLOCKS):
        last_store = BLOCK_FN[bi](srcs[bi], dsts[bi], last_store)

    P.emit("y", None, wait=P.w("hio"))
    P.barrier()
    P.build()
    return nc


_NC_CACHE = {}


def _get_nc():
    key = NBLOCKS
    if key not in _NC_CACHE:
        _NC_CACHE[key] = build_program()
    return _NC_CACHE[key]


def make_in_maps(inputs):
    base = {}
    for n, sh in WSHAPES.items():
        base[n] = np.ascontiguousarray(np.asarray(inputs[n], np.float32).reshape(sh))
    for n, ln in VSHAPES.items():
        base[n] = np.ascontiguousarray(np.asarray(inputs[n], np.float32).reshape(ln))
    for n, a in CONSTS.items():
        base[n] = a
    x = np.asarray(inputs["L0_attn_x"], np.float32)
    maps = []
    base = {("c_" + n if n in CONSTS else n): v for n, v in base.items()}
    for c in range(NCORE):
        m = dict(base)
        m["xT"] = np.ascontiguousarray(x[c].T)
        maps.append(m)
    return maps


def kernel(**inputs):
    nc = _get_nc()
    maps = make_in_maps(inputs)
    res = run_bass_kernel_spmd(nc, maps, core_ids=list(range(NCORE)))
    out = np.stack([np.asarray(r["outT"], np.float32).T for r in res.results], axis=0)
    return out


# version 19
# speedup vs baseline: 1.0692x; 1.0692x over previous
import math
import numpy as np
import ml_dtypes
import concourse.bass as bass
import concourse.mybir as mybir
from concourse.bass_utils import run_bass_kernel_spmd

F32 = mybir.dt.float32
BF16 = mybir.dt.bfloat16
AF = mybir.ActivationFunctionType
ALU = mybir.AluOpType
AX = mybir.AxisListType
BF = ml_dtypes.bfloat16

T = 2048
D = 1024
NCORE = 8
NBLOCKS = 4
RN_C = 12582912.0
TWO_PI = 2.0 * math.pi

ENGS = ("t", "v", "s", "g", "y")
ENG_ATTR = {"t": "tensor", "v": "vector", "s": "scalar", "g": "gpsimd", "y": "sync"}


class Prog:

    def __init__(self, nc):
        self.nc = nc
        self.recs = {e: [] for e in ENGS}
        self.sems = {}
        self.c = {}
        self._sem_cms = []
        self.bank_free = {}

    def sem(self, name):
        cm = self.nc.semaphore(name)
        h = cm.__enter__()
        self._sem_cms.append(cm)
        self.sems[name] = h
        self.c[name] = 0
        return h

    def emit(self, eng, method, *args, wait=None, waits=None, inc=None, incn=1, **kw):
        ws = list(waits) if waits else []
        if wait is not None:
            ws.append(wait)
        if inc is not None:
            self.c[inc] += incn
        self.recs[eng].append((method, args, kw, ws, inc, incn))

    def w(self, name):
        return (name, self.c[name])

    def bank_wait(self, bank):
        f = self.bank_free.pop(bank, None)
        return [f] if f else []

    def bank_used(self, bank, sem_val):
        self.bank_free[bank] = sem_val

    def barrier(self):
        tgt = self.c["bar"] + len(ENGS)
        for e in ENGS:
            self.emit(e, None, inc="bar")
        for e in ENGS:
            self.recs[e].append((None, (), {}, [("bar", tgt)], None, 1))
        self.bank_free.clear()

    def build(self):
        with self.nc.Block() as block:
            for eng in ENGS:
                recs = self.recs[eng]

                def body(E, recs=recs):
                    for method, args, kw, ws, inc, incn in recs:
                        for (sn, sv) in ws:
                            E.wait_ge(self.sems[sn], sv)
                        ins = None
                        if method is not None:
                            ins = getattr(E, method)(*args, **kw)
                        if inc is not None:
                            if ins is None:
                                ins = E.nop()
                            ins.then_inc(self.sems[inc], incn)
                getattr(block, ENG_ATTR[eng])(body)
        for cm in reversed(self._sem_cms):
            cm.__exit__(None, None, None)


def make_consts():
    c = {}

    c["ones1b"] = np.ones((1, 128), BF)
    c["colb"] = np.ones((128, 1), BF)
    c["colf"] = np.ones((128, 1), np.float32)
    c["idb"] = np.eye(128, dtype=np.float32).astype(BF)
    sw = np.zeros((128, 128), np.float32)
    for k in range(128):
        sw[k, k ^ 1] = 1.0
    c["swapb"] = sw.astype(BF)
    c["scanm"] = np.broadcast_to((np.arange(512) % 64 != 0).astype(np.float32),
                                 (128, 512)).astype(BF).copy()
    ROPE_BASE, DR = 10000.0, 64
    inv_freq = 1.0 / ROPE_BASE ** (np.arange(0, DR, 2, dtype=np.float32) / DR)
    tt = np.arange(T, dtype=np.float32)
    freqs = np.outer(tt, inv_freq)
    cos = np.concatenate([np.cos(freqs), np.cos(freqs)], -1)[:, ::2]
    sin = np.concatenate([np.sin(freqs), np.sin(freqs)], -1)[:, ::2]
    ropeC = np.zeros((128, T), np.float32)
    ropeS = np.zeros((128, T), np.float32)
    for i in range(32):
        ropeC[2 * i] = cos[:, i]
        ropeC[2 * i + 1] = cos[:, i]
        ropeS[2 * i] = -sin[:, i]
        ropeS[2 * i + 1] = sin[:, i]
    ropeC[64:] = ropeC[:64]
    ropeS[64:] = ropeS[:64]
    c["ropeC"] = ropeC.astype(BF)
    c["ropeS"] = ropeS.astype(BF)
    c["iota8"] = np.broadcast_to(np.arange(8, dtype=np.float32), (128, 8)).copy()
    c["idf"] = np.eye(16, dtype=np.float32)
    c["onesall"] = np.ones((65, 128), np.float32)
    e16 = np.zeros((128, 128), np.float32)
    for i in range(8):
        e16[0:64, 16 * i + 2 * i] = 1.0
        e16[64:128, 16 * i + 2 * i + 1] = 1.0
    c["e16"] = e16.astype(BF)
    return c


CONSTS = make_consts()

WSHAPES = {
    "L0_attn_Wk": (D, D), "L0_attn_Wv": (D, D), "L0_attn_Wq": (D, D),
    "L0_attn_Wkp": (D, D), "L0_attn_Wqp": (D, D), "L0_attn_Wo": (D, D),
    "L0_ffn_Wr": (D, 8),
    "L0_ffn_W1": (8 * 1024, 2048), "L0_ffn_W2": (8 * 2048, 1024),
    "L1_attn_w_down_q": (512, D), "L1_attn_w_down_kv": (512, D),
    "L1_attn_w_up_q": (D, 512), "L1_attn_w_up_k": (D, 512),
    "L1_attn_w_up_v": (D, 512), "L1_attn_w_out": (D, D),
    "L1_ffn_w1_w": (4096, D), "L1_ffn_w2_w": (4096, D),
    "L1_ffn_h1_w": (4096, 4096), "L1_ffn_h2_w": (4096, 4096),
    "L1_ffn_w3_w": (D, 4096),
    "L0_ffn_b1": (8, 2048), "L0_ffn_b2": (8, 1024),
}
VSHAPES = {
    "L0_attn_bk": D, "L0_attn_bv": D, "L0_attn_bq": D, "L0_attn_bkp": D,
    "L0_attn_bqp": D, "L0_attn_phase_scale": D, "L0_attn_ln_g": D,
    "L0_attn_ln_b": D, "L0_attn_bo": D, "L0_ffn_br": 8,
    "L1_attn_ln_scale": D, "L1_attn_ln_bias": D,
    "L1_ffn_w1_s": 4096, "L1_ffn_w2_s": 4096, "L1_ffn_h1_s": 4096,
    "L1_ffn_h2_s": 4096, "L1_ffn_w3_s": D,
}


def build_program():
    nc = bass.Bass("TRN2", target_bir_lowering=False)
    P = Prog(nc)

    dr = {}
    dr["xT"] = nc.dram_tensor("xT", [D, T], F32, kind="ExternalInput")
    dr["out"] = nc.dram_tensor("outT", [D, T], F32, kind="ExternalOutput")
    dr["R"] = nc.dram_tensor("Rscratch", [D, T], F32)
    for n, sh in WSHAPES.items():
        dr[n] = nc.dram_tensor(n, list(sh), F32, kind="ExternalInput")
    for n, ln in VSHAPES.items():
        dr[n] = nc.dram_tensor(n, [ln], F32, kind="ExternalInput")
    for n, a in CONSTS.items():
        dr[n] = nc.dram_tensor("c_" + n, list(a.shape), BF16 if a.dtype == BF else F32,
                               kind="ExternalInput")

    sba = nc.alloc_sbuf_tensor
    BIG1 = sba("BIG1", [128, 32768], BF16)
    BIG2 = sba("BIG2", [128, 32768], BF16)
    BIG3 = sba("BIG3", [128, 20480], BF16)
    RS = sba("RS", [128, 4096], F32)
    STAT = sba("STAT", [128, 2560], F32)
    SCRF = sba("SCRF", [128, 1024], F32)
    SQB = SCRF.bitcast(BF16)
    ROPE2 = sba("ROPE2", [128, 1024], BF16)
    QKM = sba("QKM", [128, 8], F32)
    LNP = sba("LNP", [128, 1024], BF16)
    LNPF = LNP.bitcast(F32)
    B2E = LNP
    CS = {}
    for n, a in CONSTS.items():
        if n in ("ropeC", "ropeS"):
            continue
        CS[n] = sba(n, list(a.shape), BF16 if a.dtype == BF else F32)
    PVX = sba("PVX", [128, 96], F32)
    BRROW = sba("BRROW", [1, 8], F32)
    GB8 = sba("GB8", [128, 16], F32)

    ps = nc.alloc_psum_tensor
    PSN = [ps(f"PS{i}", [128, 512], F32) for i in range(6)]
    PT = ps("PT", [128, 1024], BF16)
    PT2 = ps("PT2", [128, 1024], BF16)
    PTS = [PT, PT2]
    PTS4 = [(PT, "PT"), (PT2, "PT2"),
            (PSN[4].bitcast(BF16), 4), (PSN[5].bitcast(BF16), 5)]
    PREP_GI = {"i": 0}

    for s in ("wl", "hio", "ts", "vs", "ss", "gs", "bar", "wn0", "wn1",
              "wa0", "wa1", "wb0", "wb1", "wr2", "msn", "lnp0", "lnp1", "wpre",
              "rio", "hes", "hfs"):
        P.sem(s)

    def Rsl(i):
        return RS[:, i * 512:(i + 1) * 512]

    def gload(dst_ap, src_ap, waits=None, slow=False, sem="wl"):
        kw = {"allow_slow_non_contiguous": True} if slow else {}
        P.emit("g", "dma_start", dst_ap, src_ap, waits=waits, inc=sem, incn=16, **kw)
        return P.w(sem)

    def hdma(dst_ap, src_ap, waits=None, sem="hio"):
        P.emit("y", "dma_start", dst_ap, src_ap, waits=waits, inc=sem, incn=16)
        return P.w(sem)

    for n in CONSTS:
        if n in ("ropeC", "ropeS"):
            continue
        gload(CS[n][:, :], dr[n][:, :])
    VCOL = {}
    ncol = 0
    for n, short in (("L0_attn_bk", "bk"), ("L0_attn_bv", "bv"), ("L0_attn_bq", "bq"),
                     ("L0_attn_bkp", "bkp"), ("L0_attn_bqp", "bqp"),
                     ("L0_attn_phase_scale", "ps"), ("L0_attn_ln_g", "lng"),
                     ("L0_attn_ln_b", "lnb"), ("L0_attn_bo", "bo"),
                     ("L1_attn_ln_scale", "l1s"), ("L1_attn_ln_bias", "l1b")):
        ap = bass.AP(dr[n], 0, [[1, 128], [128, 8]])
        w_pv = gload(PVX[:, ncol:ncol + 8], ap, slow=True)
        VCOL[short] = ncol
        ncol += 8
    VCOL["ps2"] = ncol
    ncol += 8

    def pv(short, m):
        return PVX[:, VCOL[short] + m: VCOL[short] + m + 1]

    WR32 = sba("WR32", [128, 64], F32)
    for i in range(8):
        w_wr = gload(WR32[:, i * 8:(i + 1) * 8], dr["L0_ffn_Wr"][i * 128:(i + 1) * 128, :])
    B1C = sba("B1C", [128, 128], F32)
    for e in range(8):
        ap = bass.AP(dr["L0_ffn_b1"], 2048 * e, [[1, 128], [128, 16]])
        gload(B1C[:, e * 16:(e + 1) * 16], ap, slow=True)
    w_br = gload(BRROW[:, :], bass.AP(dr["L0_ffn_br"], 0, [[1, 1], [1, 8]]))
    SCL = sba("SCL", [128, 144], F32)
    SCOL = {}
    off = 0
    for n, nch in (("L1_ffn_w1_s", 32), ("L1_ffn_w2_s", 32), ("L1_ffn_h1_s", 32),
                   ("L1_ffn_h2_s", 32), ("L1_ffn_w3_s", 8)):
        ap = bass.AP(dr[n], 0, [[1, 128], [128, nch]])
        gload(SCL[:, off:off + nch], ap, slow=True)
        SCOL[n] = off
        off += nch

    def scol(name, m):
        return SCL[:, SCOL[name] + m: SCOL[name] + m + 1]

    consts_w = P.w("wl")
    P.emit("v", "tensor_scalar_mul", PVX[:, VCOL["ps2"]:VCOL["ps2"] + 8],
           PVX[:, VCOL["ps"]:VCOL["ps"] + 8], 1.0 / TWO_PI, wait=consts_w, inc="vs")
    P.emit("v", "drain")
    P.emit("t", None, wait=consts_w)
    P.emit("s", None, wait=consts_w)

    def layer_norm(src, dst, cslot, eps, add_to=None, fp32_stats=False,
                   gb=None, src_is_bf=False, src_wait=None, rowbase=0):
        bS, bQ = 3, 4
        if src_wait is None:
            swf = lambda i: None
        elif callable(src_wait):
            swf = src_wait
        else:
            swf = lambda i, s=src_wait: (s if i == 0 else None)
        twS = P.bank_wait(bS)
        twQ = P.bank_wait(bQ)
        cast_w = [None] * 8
        for i in range(8):
            if fp32_stats:
                P.emit("t", "matmul", PSN[bS][0:1, 0:512], CS["colf"][:, :], src(i),
                       start=(i == 0), stop=(i == 7),
                       waits=(twS if i == 0 else []) +
                             ([swf(i)] if swf(i) else []),
                       inc="ts" if i == 7 else None)
            else:
                xb = src(i) if src_is_bf else cslot(i)
                if not src_is_bf:
                    P.emit("v", "tensor_copy", xb, src(i),
                           waits=[swf(i)] if swf(i) else [],
                           inc="vs")
                    P.emit("v", "drain")
                    cast_w[i] = P.w("vs")
                P.emit("t", "matmul", PSN[bS][0:1, 0:512], CS["colb"][:, :], xb,
                       start=(i == 0), stop=(i == 7),
                       waits=([P.w("vs")] if not src_is_bf else []) +
                             (twS if i == 0 else []) +
                             ([swf(i)] if (src_is_bf and swf(i)) else []),
                       inc="ts" if i == 7 else None)
        sum_done = P.w("ts")
        for i in range(8):
            xb = src(i) if (src_is_bf or fp32_stats) else cslot(i)
            sq = SQB[:, (i % 2) * 512:(i % 2) * 512 + 512] if not fp32_stats else \
                SCRF[:, (i % 2) * 512:(i % 2) * 512 + 512]
            sqw = []
            if cast_w[i] is not None:
                sqw.append(cast_w[i])
            elif swf(i):
                sqw.append(swf(i))
            if i >= 2:
                sqw.append(P.w("ts"))
            P.emit("s", "activation", sq, xb, AF.Square, waits=sqw, inc="ss")
            if fp32_stats:
                P.emit("t", "matmul", PSN[bQ][0:1, 0:512], CS["colf"][:, :], sq,
                       start=(i == 0), stop=(i == 7),
                       waits=[P.w("ss")] + (twQ if i == 0 else []), inc="ts")
            else:
                P.emit("t", "matmul", PSN[bQ][0:1, 0:512], CS["colb"][:, :], sq,
                       start=(i == 0), stop=(i == 7),
                       waits=[P.w("ss")] + (twQ if i == 0 else []), inc="ts")
        sq_done = P.w("ts")
        MU = STAT[0:1, rowbase:rowbase + 512]
        RSr = STAT[0:1, rowbase + 512:rowbase + 1024]
        P.emit("v", "tensor_scalar_mul", MU, PSN[bS][0:1, 0:512],
               1.0 / 1024.0, waits=[sum_done, sq_done], inc="vs")
        P.emit("v", "drain")
        P.emit("v", "tensor_scalar_mul", PSN[bQ][0:1, 0:512], PSN[bQ][0:1, 0:512],
               1.0 / 1024.0)
        P.emit("v", "drain")
        P.emit("v", "tensor_tensor", RSr, MU, MU, ALU.mult)
        P.emit("v", "drain")
        P.emit("v", "tensor_sub", PSN[bQ][0:1, 0:512], PSN[bQ][0:1, 0:512], RSr)
        P.emit("v", "drain")
        P.emit("v", "tensor_scalar_add", PSN[bQ][0:1, 0:512], PSN[bQ][0:1, 0:512],
               float(eps))
        P.emit("v", "drain")
        P.emit("v", "engine_nop", inc="vs")
        P.emit("s", "activation", RSr, PSN[bQ][0:1, 0:512], AF.Sqrt,
               wait=P.w("vs"), inc="ss")
        P.emit("v", "reciprocal", RSr, RSr, wait=P.w("ss"), inc="vs")
        P.emit("v", "drain")
        rows_done = P.w("vs")
        P.emit("t", "matmul", PSN[bS][:, 0:512], CS["onesall"][0:1, :], MU,
               start=True, stop=True, waits=[rows_done], inc="ts")
        P.emit("t", "matmul", PSN[bQ][:, 0:512], CS["onesall"][0:1, :], RSr,
               start=True, stop=True, inc="ts")
        bc_done = P.w("ts")
        for i in range(8):
            tmp = SCRF[:, 0:512]
            P.emit("v", "tensor_sub", tmp, src(i), PSN[bS][:, 0:512],
                   waits=[bc_done] if i == 0 else [])
            P.emit("v", "drain")
            if add_to is not None:
                ln32 = SCRF[:, 512:1024]
                P.emit("v", "tensor_mul", ln32, tmp, PSN[bQ][:, 0:512])
                P.emit("v", "drain")
                P.emit("v", "tensor_add", add_to(i), add_to(i), ln32)
                P.emit("v", "drain")
                P.emit("v", "tensor_copy", dst(i), ln32,
                       inc="vs" if (i == 7 and gb is None) else None)
                P.emit("v", "drain")
            else:
                P.emit("v", "tensor_mul", dst(i), tmp, PSN[bQ][:, 0:512],
                       inc="vs" if (i == 7 and gb is None) else None)
                P.emit("v", "drain")
        if gb is None:
            dw = P.w("vs")
            P.bank_used(bS, dw)
            P.bank_used(bQ, dw)
            return dw, dw
        P.emit("v", "engine_nop", inc="vs")
        nv = P.w("vs")
        P.bank_used(bS, nv)
        P.bank_used(bQ, nv)
        sc, bc = gb
        for i in range(8):
            P.emit("s", "activation", dst(i), dst(i), AF.Identity,
                   scale=sc(i), bias=bc(i), waits=[nv] if i == 0 else [],
                   inc="ss" if i == 7 else None)
            P.emit("s", "drain")
        dw = P.w("ss")
        return dw, nv

    PT2F = PT2.bitcast(F32)
    PTF0 = PT.bitcast(F32)
    LNP_FREE = {"w": []}

    def ln_prefetch(src_dram, ts0, slot, fp32=False, rs2_row=None, eps=1e-5,
                    entry=None, banks=None):
        if banks is None:
            banks = (("PT2", PT2F), ("PT", PT.bitcast(F32)))
        (bkS, apS), (bkQ, apQ) = banks
        state = {"dma": {}}

        assert not fp32, "fp32 prefetch unsupported (slot budget)"

        def chunk_dma(i, w):
            sl_ = (i // 2) % 2
            state["dma"][i] = state["dma"][i + 1] = gload(
                LNP[:, 0:1024],
                bass.AP(src_dram, ts0 + (i * 128) * 2048,
                        [[2048, 128], [128 * 2048, 2], [1, 512]]),
                sem=f"lnp{sl_}", waits=(w or []) + LNP_FREE["w"])

        def dma_issue():
            chunk_dma(0, [entry] if entry else None)

        def mms(row_free=None):
            MUr = STAT[0:1, slot * 1024: slot * 1024 + 512]
            RSr = STAT[0:1, slot * 1024 + 512: slot * 1024 + 1024]
            colv = CS["colf"] if fp32 else CS["colb"]
            twS = P.bank_wait(bkS)
            twQ = P.bank_wait(bkQ)
            for i in range(8):
                sl_ = i % 2
                xb = LNP[:, sl_ * 512:(sl_ + 1) * 512]
                P.emit("t", "matmul", apS[0:1, 0:512], colv[:, :], xb,
                       start=(i == 0), stop=(i == 7),
                       waits=(twS if i == 0 else []) + [state["dma"][i]],
                       inc="ts")
                mms_i = P.w("ts")
                P.emit("s", "activation", xb, xb, AF.Square,
                       waits=[mms_i], inc="ss")
                sq_done = P.w("ss")
                P.emit("t", "matmul", apQ[0:1, 0:512], colv[:, :], xb,
                       start=(i == 0), stop=(i == 7),
                       waits=[sq_done] + (twQ if i == 0 else []), inc="ts")
                mmq_i = P.w("ts")
                if i % 2 == 1 and i + 1 < 8:
                    chunk_dma(i + 1, [mmq_i, mms_i])
            P.emit("v", "tensor_scalar_mul", MUr, apS[0:1, 0:512], 1.0 / 1024.0,
                   waits=[P.w("ts")] + (row_free or []), inc="vs")
            P.emit("v", "drain")
            P.bank_used(bkS, P.w("vs"))
            P.emit("v", "tensor_scalar_mul", apQ[0:1, 0:512], apQ[0:1, 0:512],
                   1.0 / 1024.0)
            P.emit("v", "drain")
            P.emit("v", "tensor_tensor", RSr, MUr, MUr, ALU.mult)
            P.emit("v", "drain")
            P.emit("v", "tensor_sub", apQ[0:1, 0:512], apQ[0:1, 0:512], RSr)
            P.emit("v", "drain")
            if rs2_row is not None:
                P.emit("v", "tensor_scalar", rs2_row, apQ[0:1, 0:512], 1.0 + eps,
                       eps * eps, ALU.mult, ALU.add)
                P.emit("v", "drain")
                P.emit("v", "engine_nop", inc="vs")
                P.emit("s", "activation", rs2_row, rs2_row, AF.Sqrt,
                       wait=P.w("vs"), inc="ss")
                P.emit("v", "reciprocal", rs2_row, rs2_row, wait=P.w("ss"))
                P.emit("v", "drain")
            P.emit("v", "tensor_scalar_add", apQ[0:1, 0:512], apQ[0:1, 0:512],
                   float(eps))
            P.emit("v", "drain")
            P.emit("v", "engine_nop", inc="vs")
            P.emit("s", "activation", RSr, apQ[0:1, 0:512], AF.Sqrt,
                   wait=P.w("vs"), inc="ss")
            P.emit("v", "reciprocal", RSr, RSr, wait=P.w("ss"), inc="vs")
            P.emit("v", "drain")
            P.bank_used(bkQ, P.w("vs"))
            LNP_FREE["w"] = [P.w("ts")]
            return P.w("vs")

        return dma_issue, mms

    def ln_apply(slot, src, dst, rows_done, src_wait=None, add_to=None, gb=None,
                 rs2_row=None, dst2=None, gb2=None, banks=None):
        if banks is None:
            banks = ((4, PSN[4]), (5, PSN[5]), (3, PSN[3]))
        (kM, apM), (kR, apR), (kR2, apR2) = banks
        MUr = STAT[0:1, slot * 1024: slot * 1024 + 512]
        RSr = STAT[0:1, slot * 1024 + 512: slot * 1024 + 1024]
        tw4 = P.bank_wait(kM)
        tw5 = P.bank_wait(kR)
        P.emit("t", "matmul", apM[:, 0:512], CS["onesall"][0:1, :], MUr,
               start=True, stop=True, waits=tw4 + [rows_done], inc="ts")
        P.emit("t", "matmul", apR[:, 0:512], CS["onesall"][0:1, :], RSr,
               start=True, stop=True, waits=tw5, inc="ts")
        if rs2_row is not None:
            tw3 = P.bank_wait(kR2)
            P.emit("t", "matmul", apR2[:, 0:512], CS["onesall"][0:1, :], rs2_row,
                   start=True, stop=True, waits=tw3, inc="ts")
        bc = P.w("ts")
        ln_apply.last_bc = bc
        slotw = {}
        sw_l = ([] if src_wait is None else
                (list(src_wait) if isinstance(src_wait, list) else [src_wait]))
        for i in range(8):
            s = SCRF[:, (i % 2) * 512:(i % 2) * 512 + 512]
            ws = ([bc] if i == 0 else []) + (sw_l if i == 0 else []) + \
                 (slotw.get(i - 2, []))
            P.emit("v", "tensor_sub", s, src(i), apM[:, 0:512], waits=ws)
            P.emit("v", "drain")
            if rs2_row is not None:
                P.emit("v", "tensor_mul", dst2(i), s, apR2[:, 0:512])
                P.emit("v", "drain")
                P.emit("v", "engine_nop", inc="vs")
                sc, bcn = gb2
                P.emit("s", "activation", dst2(i), dst2(i), AF.Identity,
                       scale=sc(i), bias=bcn(i), wait=P.w("vs"), inc="ss")
                P.emit("s", "drain")
            P.emit("v", "tensor_mul", s, s, apR[:, 0:512], inc="vs")
            P.emit("v", "drain")
            mv = P.w("vs")
            myw = []
            if add_to is not None:
                P.emit("g", "tensor_add", add_to(i), add_to(i), s, wait=mv,
                       inc="gs")
                myw.append(P.w("gs"))
            if dst is not None:
                P.emit("s", "activation", dst(i), s,
                       AF.Identity if gb is None else AF.Identity,
                       scale=(gb[0](i) if gb else 1.0),
                       bias=(gb[1](i) if gb else 0.0), wait=mv, inc="ss")
                P.emit("s", "drain")
                myw.append(P.w("ss"))
            if not myw:
                myw = [mv]
            slotw[i] = list(myw)
        nv = P.w("vs")
        P.bank_used(kM, nv)
        P.bank_used(kR, nv)
        dst_w = P.w("ss") if (dst is not None or rs2_row is not None) else nv
        aux_w = P.w("gs") if add_to is not None else nv
        if add_to is not None:
            P.emit("v", None, wait=aux_w)
            P.emit("s", None, wait=aux_w)
        if dst is not None or rs2_row is not None:
            P.emit("v", None, wait=P.w("ss"))
        if rs2_row is not None:
            P.bank_used(kR2, nv)
        return dst_w, aux_w

    def prep_wT(dram_w, rows, cols, dst_pool, dst_off, natbuf, natw=None,
                evac="v", hist=None, nslots=2, sem=None, pre=None, deep=False):
        nr, ncch = rows // 128, cols // 128
        last = None
        glist = []
        if hist is None:
            hist = prep_hist
        for r in range(nr):
            if pre is not None:
                nslot = pre[0](r)[:, 0:cols]
                wl = pre[1] if r == 0 else None
            else:
                slot = len(hist) % nslots
                nslot = natbuf[slot][:, 0:cols]
                sm = sem[slot % 2] if isinstance(sem, tuple) else \
                    (sem or f"wn{slot % 2}")
                wl = gload(nslot, dram_w[r * 128:(r + 1) * 128, :],
                           sem=sm,
                           waits=[hist[-nslots]] if len(hist) >= nslots else None)
            for gi, g0 in enumerate(range(0, ncch, 8)):
                gn = min(8, ncch - g0)
                if deep:
                    ptile, pbk = PTS4[PREP_GI["i"] % 4]
                    PREP_GI["i"] += 1
                else:
                    pb = gi % 2
                    ptile = PTS[pb]
                    pbk = "PT" if pb == 0 else "PT2"
                tw = P.bank_wait(pbk)
                for j in range(gn):
                    c = g0 + j
                    wts = (tw if j == 0 else []) + \
                          ([wl] if (j == 0 and wl is not None) else [])
                    P.emit("t", "transpose", ptile[:, j * 128:(j + 1) * 128],
                           nslot[:, c * 128:(c + 1) * 128], CS["idb"][:, :],
                           waits=wts, inc="ts" if j == gn - 1 else None)
                tv = P.w("ts")
                esem = "vs" if evac == "v" else "ss"
                meth = "tensor_copy" if evac == "v" else "copy"
                if rows == 128:
                    o = dst_off + g0 * 128
                    P.emit(evac, meth, dst_pool[:, o:o + gn * 128],
                           ptile[:, 0:gn * 128], waits=[tv], inc=esem)
                else:
                    for j in range(gn):
                        c = g0 + j
                        o = dst_off + c * rows + r * 128
                        P.emit(evac, meth, dst_pool[:, o:o + 128],
                               ptile[:, j * 128:(j + 1) * 128],
                               waits=[tv] if j == 0 else [],
                               inc=esem if j == gn - 1 else None)
                if evac == "v":
                    P.emit("v", "drain")
                else:
                    P.emit("s", "drain")
                last = P.w(esem)
                P.bank_used(pbk, last)
                glist.append(last)
            if pre is None:
                hist.append(P.w("ts"))
        prep_wT.last_groups = glist
        return last

    prep_hist = []
    PRELD = {}

    def block_l0_attn(src_dram, dst_dram, entry_store=None):
        natb = [BIG3[:, 16384:17408], BIG3[:, 17408:18432]]
        pf = {}
        di0, mm0 = ln_prefetch(src_dram, 0, 0, entry=entry_store)
        di0()
        pf[0] = mm0()
        wR = hdma(RS[:, :],
                  bass.AP(src_dram, 0, [[2048, 128], [128 * 2048, 8], [1, 512]]),
                  waits=[entry_store] if entry_store else None)
        dstw, radd_w = ln_apply(0, lambda i: Rsl(i),
                                lambda i: BIG1[:, i * 512:(i + 1) * 512],
                                pf[0], src_wait=wR, add_to=lambda i: Rsl(i))
        es_w = hdma(bass.AP(dst_dram, 0, [[2048, 128], [128 * 2048, 8], [1, 512]]),
                    RS[:, :], waits=[radd_w], sem="hes")
        kq_g_free = None
        WT = {}
        WTW = {}
        for wi, (name, pool, offc) in enumerate((("Wq", BIG2, 0), ("Wqp", BIG2, 24576),
                                                 ("Wk", BIG2, 8192), ("Wkp", BIG2, 16384),
                                                 ("Wv", BIG3, 0), ("Wo", BIG3, 8192))):
            WTW[name] = prep_wT(dr["L0_attn_" + name], D, D, pool, offc, natb,
                                evac="v" if wi % 2 == 0 else "s")
            WT[name] = (pool, offc)

        def wslice(name, i, m):
            pool, offc = WT[name]
            return pool[:, offc + i * 1024 + m * 128: offc + i * 1024 + m * 128 + 128]

        def sl(k):
            return BIG1[:, k * 512:(k + 1) * 512]

        LN0, KQ, CKP, SKP, CQP, SQP, MR, MI = 0, 8, 16, 24, 32, 40, 48, 56
        RET, ZG = 32, 40

        def proj(wname, rhs_slot_fn, out_cb, bank0, extra_w=None):
            for m in range(8):
                b = (m % 4)
                tw = P.bank_wait(b)
                for i in range(8):
                    P.emit("t", "matmul", PSN[b][:, 0:512], wslice(wname, i, m),
                           rhs_slot_fn(i), start=(i == 0), stop=(i == 7),
                           waits=(tw if i == 0 else []) +
                                 (extra_w if (m == 0 and i == 0 and extra_w) else []),
                           inc="ts" if i == 7 else None)
                out_cb(m, PSN[b], b, P.w("ts"))

        RB2F = BIG1.bitcast(F32)
        L0A_BANKS = ((5, PSN[5]), ("PT", PTF0), (4, PSN[4]))

        def SAP(t0):
            return bass.AP(src_dram, t0, [[2048, 128], [128 * 2048, 8], [1, 512]])

        def RAP(t0):
            return bass.AP(dst_dram, t0, [[2048, 128], [128 * 2048, 8], [1, 512]])

        prev_fstore = None
        for tt in range(4):
            ts0 = tt * 512
            if tt < 3:
                dii, mmi = ln_prefetch(src_dram, (tt + 1) * 512, (tt + 1) % 2,
                                       entry=entry_store)
                dii()

            def ev_q(m, pb, bank, tsw):
                P.emit("s", "activation", sl(KQ + m), pb[:, 0:512], AF.Identity,
                       bias=pv("bq", m),
                       waits=[tsw] + ([kq_g_free] if (m == 0 and kq_g_free)
                                      else []), inc="ss")
                P.emit("s", "drain")
                P.bank_used(bank, P.w("ss"))
            proj("Wq", lambda i: sl(LN0 + i), ev_q, 0,
                 extra_w=[dstw] + ([WTW["Wq"]] if tt == 0 else []))
            q_done = P.w("ss")

            def phase_pass(wname, bias_name, cslot0, sslot0, rhs0, first_wait=None,
                           trig_wait=None):
                def Aslot(m):
                    return SCRF[:, (m % 2) * 512:(m % 2) * 512 + 512]

                def Tp(m):
                    return PSN[4 + m % 2][:, 0:512]

                def Bslot(m):
                    return BIG3[:, 18432 + (m % 2) * 512: 18432 + (m % 2) * 512 + 512]

                vchain = {}

                def emit_tanh(m, b, tsw):
                    ws = [tsw] + ([vchain[m - 2]] if m >= 2 else [])
                    P.emit("s", "activation", Aslot(m), PSN[b][:, 0:512], AF.Tanh,
                           bias=pv(bias_name, m), waits=ws, inc="ss")
                    P.emit("s", "drain")
                    P.bank_used(b, P.w("ss"))
                    return P.w("ss")

                trig_w = {}

                def emit_trig(m):
                    P.emit("s", "activation", sl(sslot0 + m), Aslot(m), AF.Sin,
                           scale=TWO_PI,
                           waits=[vchain[m]] +
                                 ([trig_wait] if (m == 0 and trig_wait) else []))
                    P.emit("s", "drain")
                    P.emit("s", "activation", sl(cslot0 + m), Tp(m), AF.Sin,
                           scale=TWO_PI, inc="ss")
                    P.emit("s", "drain")
                    P.bank_used(4 + m % 2, P.w("ss"))
                    trig_w[m] = P.w("ss")

                def emit_vchain(m, thw):
                    A = Aslot(m)
                    T = Tp(m)
                    Bb = Bslot(m)
                    tw = P.bank_wait(4 + m % 2)
                    P.emit("v", "tensor_scalar_mul", A, A, pv("ps2", m),
                           waits=[thw] + tw)
                    P.emit("v", "drain")
                    P.emit("v", "tensor_scalar_add", T, A, RN_C)
                    P.emit("v", "drain")
                    P.emit("v", "tensor_scalar_add", Bb, T, -RN_C)
                    P.emit("v", "drain")
                    P.emit("v", "tensor_sub", A, A, Bb)
                    P.emit("v", "drain")
                    P.emit("v", "tensor_scalar_add", T, A, 0.25)
                    P.emit("v", "drain")
                    P.emit("v", "tensor_scalar", Bb, T, 0.5, None, ALU.is_ge)
                    P.emit("v", "drain")
                    P.emit("v", "tensor_sub", T, T, Bb)
                    P.emit("v", "drain")
                    P.emit("v", "engine_nop", inc="vs")
                    vchain[m] = P.w("vs")

                for m in range(8):
                    b = m % 4
                    tw = P.bank_wait(b)
                    for i in range(8):
                        P.emit("t", "matmul", PSN[b][:, 0:512], wslice(wname, i, m),
                               sl(rhs0 + i), start=(i == 0), stop=(i == 7),
                               waits=(tw if i == 0 else []) +
                                     ((first_wait if isinstance(first_wait, list)
                                       else [first_wait])
                                      if (m == 0 and i == 0 and first_wait)
                                      else []),
                               inc="ts" if i == 7 else None)
                    thw = emit_tanh(m, b, P.w("ts"))
                    if m >= 1:
                        emit_trig(m - 1)
                    emit_vchain(m, thw)
                emit_trig(7)
                phase_pass.trig_w = trig_w
                return P.w("ss")

            qp_done = phase_pass("Wqp", "bqp", CQP, SQP, KQ,
                                 first_wait=[q_done] +
                                 ([WTW["Wqp"]] if tt == 0 else []))
            if tt < 3:
                pf[tt + 1] = mmi(row_free=[ln_apply.last_bc])
            def ev_k(m, pb, bank, tsw):
                P.emit("s", "activation", sl(KQ + m), pb[:, 0:512], AF.Identity,
                       bias=pv("bk", m), waits=[tsw, qp_done] if m == 0 else [tsw],
                       inc="ss")
                P.emit("s", "drain")
                P.bank_used(bank, P.w("ss"))
            proj("Wk", lambda i: sl(LN0 + i), ev_k, 0,
                 extra_w=[WTW["Wk"]] if tt == 0 else None)
            key_done = P.w("ss")
            kp_done = phase_pass("Wkp", "bkp", CKP, SKP, KQ,
                                 first_wait=[key_done] +
                                 ([WTW["Wkp"]] if tt == 0 else []),
                                 trig_wait=prev_fstore)
            kp_trig = dict(phase_pass.trig_w)

            def ev_v(m, pb, bank, tsw):
                P.emit("s", "activation", sl(KQ + m), pb[:, 0:512], AF.Identity,
                       bias=pv("bv", m), waits=[tsw], inc="ss")
                P.emit("s", "drain")
                P.bank_used(bank, P.w("ss"))
                sv = P.w("ss")
                P.emit("g", "tensor_mul", sl(MR + m), sl(KQ + m), sl(CKP + m),
                       waits=[sv, kp_trig[m]])
                P.emit("g", "tensor_mul", sl(MI + m), sl(KQ + m), sl(SKP + m),
                       inc="gs")
                gw = P.w("gs")
                P.emit("v", "tensor_tensor_scan", sl(MR + m), CS["scanm"][:, :],
                       sl(MR + m), 0.0, ALU.mult, ALU.add, waits=[gw])
                P.emit("v", "drain")
                P.emit("v", "tensor_tensor_scan", sl(MI + m), CS["scanm"][:, :],
                       sl(MI + m), 0.0, ALU.mult, ALU.add)
                P.emit("v", "drain")
                P.emit("v", "tensor_mul", sl(MR + m), sl(MR + m), sl(CQP + m))
                P.emit("v", "drain")
                P.emit("v", "tensor_mul", sl(MI + m), sl(MI + m), sl(SQP + m))
                P.emit("v", "drain")
                P.emit("v", "tensor_add", sl(RET + m), sl(MR + m), sl(MI + m),
                       inc="vs")
                P.emit("v", "drain")
            vs_base = P.c["vs"]
            proj("Wv", lambda i: sl(LN0 + i), ev_v, 0,
                 extra_w=[WTW["Wv"]] if tt == 0 else None)
            ret_done = P.w("vs")
            kq_g_free = P.w("gs")
            lnx_free = P.w("ts")

            dzw, _ = layer_norm(lambda i: sl(RET + i), lambda i: sl(ZG + i),
                                None, 1e-5 * 1024.0, src_is_bf=True,
                                gb=(lambda i: pv("lng", i), lambda i: pv("lnb", i)),
                                src_wait=lambda i: ("vs", vs_base + i + 1),
                                rowbase=(tt % 2) * 1024)

            wRB = hdma(RB2F[:, 4096:8192], RAP(ts0), waits=[es_w, ret_done],
                       sem="rio")
            if tt < 3:
                wR2 = hdma(RS[:, :], SAP(ts0 + 512), waits=[radd_w, es_w])
                dstw, radd_w = ln_apply((tt + 1) % 2, lambda i: Rsl(i),
                                        lambda i: sl(LN0 + i), pf[tt + 1],
                                        src_wait=[wR2, lnx_free],
                                        add_to=lambda i: Rsl(i),
                                        banks=L0A_BANKS)
                es_w = hdma(RAP(ts0 + 512), RS[:, :], waits=[radd_w],
                            sem="hes")

            def RB2l(m):
                return RB2F[:, 4096 + m * 512: 4096 + (m + 1) * 512]

            def ev_o(m, pb, bank, tsw, wRB=wRB):
                P.emit("v", "scalar_tensor_tensor", RB2l(m), pb[:, 0:512],
                       pv("bo", m), RB2l(m), ALU.add, ALU.add,
                       waits=[tsw] + ([wRB] if m == 0 else []), inc="vs")
                P.emit("v", "drain")
                P.bank_used(bank, P.w("vs"))
            proj("Wo", lambda i: sl(ZG + i), ev_o, 0,
                 extra_w=[dzw] + ([WTW["Wo"]] if tt == 0 else []))
            wout_done = P.w("vs")
            prev_fstore = hdma(RAP(ts0), RB2F[:, 4096:8192],
                               waits=[wout_done], sem="hfs")
        P.barrier()
        return prev_fstore

    def block_l0_ffn(src_dram, dst_dram, entry_store=None):
        def lnx(i, ts0=None, w=2048):
            if ts0 is None:
                return BIG1[:, i * 2048: i * 2048 + 2048]
            return BIG1[:, i * 2048 + ts0: i * 2048 + ts0 + w]

        def ysl(i, ts0):
            return BIG1[:, 16384 + i * 2048 + ts0: 16384 + i * 2048 + ts0 + 512]

        def hmsl(m):
            return BIG3[:, m * 512:(m + 1) * 512]

        BC = BIG3[:, 8192:8704]
        GMT = BIG3[:, 8704:8832]
        GMR = BIG3[:, 8832:10880]

        b = 5
        tw = P.bank_wait(b)
        for i in range(8):
            P.emit("t", "matmul", PSN[b][0:1, 0:8], CS["colf"][:, :],
                   WR32[:, i * 8:(i + 1) * 8], start=(i == 0), stop=(i == 7),
                   waits=tw if i == 0 else [], inc="ts" if i == 7 else None)
        P.emit("v", "tensor_scalar_mul", STAT[0:1, 1024:1032], PSN[b][0:1, 0:8],
               -1.0, wait=P.w("ts"), inc="vs")
        P.emit("v", "drain")
        P.emit("t", "matmul", PSN[b][:, 0:8], CS["onesall"][0:1, :],
               STAT[0:1, 1024:1032], start=True, stop=True, wait=P.w("vs"), inc="ts")
        P.emit("v", "tensor_copy", GB8[:, 0:8], PSN[b][:, 0:8], wait=P.w("ts"),
               inc="vs")
        P.emit("v", "drain")
        P.emit("t", "matmul", PSN[b][:, 0:8], CS["onesall"][0:1, :], BRROW[:, :],
               start=True, stop=True, wait=P.w("vs"), inc="ts")
        P.emit("v", "tensor_copy", GB8[:, 8:16], PSN[b][:, 0:8], wait=P.w("ts"),
               inc="vs")
        P.emit("v", "drain")
        P.bank_used(b, P.w("vs"))

        prev_store = entry_store
        lnx_done = []
        rs_free = [prev_store] if prev_store else []
        for tt in range(4):
            ts0 = tt * 512
            wR = hdma(RS[:, :],
                      bass.AP(src_dram, ts0, [[2048, 128], [128 * 2048, 8], [1, 512]]),
                      waits=rs_free or None)
            dw, _ = layer_norm(lambda i: Rsl(i), lambda i, ts0=ts0: lnx(i, ts0, 512),
                               None, 1e-5, fp32_stats=True, src_wait=wR)
            lnx_done.append(dw)
            MU = STAT[0:1, 0:512]
            RSr = STAT[0:1, 512:1024]
            MRS = STAT[0:1, 1024:1536]
            P.emit("v", "tensor_tensor", MRS, MU, RSr, ALU.mult, waits=[dw], inc="vs")
            P.emit("v", "drain")
            rows_w = P.w("vs")
            for tc in range(4):
                c0 = tc * 128
                b5 = 5
                tw5 = P.bank_wait(b5)
                for i in range(8):
                    P.emit("t", "matmul", PSN[b5][:, 0:8],
                           Rsl(i)[:, c0:c0 + 128], WR32[:, i * 8:(i + 1) * 8],
                           start=(i == 0), stop=(i == 7),
                           waits=(tw5 if i == 0 else []) + ([rows_w] if i == 0 else []),
                           inc="ts" if i == 7 else None)
                raw_w = P.w("ts")
                b6 = 2
                tw6 = P.bank_wait(b6)
                P.emit("t", "transpose", PSN[b6][:, 0:1], RSr[0:1, c0:c0 + 128],
                       CS["idf"][0:1, 0:1], waits=tw6)
                P.emit("t", "transpose", PSN[b6][:, 1:2], MRS[0:1, c0:c0 + 128],
                       CS["idf"][0:1, 0:1], inc="ts")
                P.emit("v", "tensor_copy", SCRF[:, 0:2], PSN[b6][:, 0:2],
                       wait=P.w("ts"), inc="vs")
                P.emit("v", "drain")
                P.bank_used(b6, P.w("vs"))
                rs_c = SCRF[:, 0:1]
                mrs_c = SCRF[:, 1:2]
                gl = SCRF[:, 8:16]
                P.emit("v", "tensor_scalar", gl, PSN[b5][:, 0:8], rs_c, None,
                       ALU.mult, waits=[raw_w], inc="vs")
                P.emit("v", "drain")
                P.bank_used(b5, P.w("vs"))
                P.emit("v", "scalar_tensor_tensor", gl, GB8[:, 0:8], mrs_c, gl,
                       ALU.mult, ALU.add)
                P.emit("v", "drain")
                P.emit("v", "tensor_add", gl, gl, GB8[:, 8:16])
                P.emit("v", "drain")
                mx = SCRF[:, 4:5]
                nmx = SCRF[:, 5:6]
                ssum = SCRF[:, 2:3]
                ginv = SCRF[:, 3:4]
                mn = SCRF[:, 6:7]
                et = SCRF[:, 16:24]
                msk = SCRF[:, 24:32]
                P.emit("v", "reduce_max", mx, gl, AX.X)
                P.emit("v", "drain")
                P.emit("v", "tensor_scalar_mul", nmx, mx, -1.0)
                P.emit("v", "drain")
                P.emit("v", "engine_nop", inc="vs")
                P.emit("s", "activation", et, gl, AF.Exp, bias=nmx,
                       accum_out=ssum, wait=P.w("vs"), inc="ss")
                P.emit("s", "drain")
                P.emit("v", "reciprocal", ginv, ssum, wait=P.w("ss"))
                P.emit("v", "drain")
                P.emit("v", "tensor_scalar", msk, gl, mx, None, ALU.is_ge)
                P.emit("v", "drain")
                P.emit("v", "tensor_scalar", msk, msk, -1000.0, 1000.0,
                       ALU.mult, ALU.add)
                P.emit("v", "drain")
                P.emit("v", "tensor_add", msk, msk, CS["iota8"][:, :])
                P.emit("v", "drain")
                P.emit("v", "tensor_reduce", mn, msk, AX.X, ALU.min)
                P.emit("v", "drain")
                P.emit("v", "tensor_scalar", msk, msk, mn, None, ALU.is_equal)
                P.emit("v", "drain")
                P.emit("v", "tensor_scalar", GMT[:, (tt * 4 + tc) * 8:
                                                 (tt * 4 + tc) * 8 + 8],
                       msk, ginv, None, ALU.mult, inc="vs")
                P.emit("v", "drain")
            rs_free = [P.w("ts"), P.w("vs")]
        gmt_done = P.w("vs")
        lnx_all = lnx_done[-1]

        wa = None
        wb = None
        h_end = {}
        w2_end = {}
        ye7 = {}
        b2w = {}
        for e in range(8):
            gw1 = [h_end[e - 1]] if e >= 1 else None
            for i in range(8):
                wa = gload(BIG2[:, i * 2048:(i + 1) * 2048],
                           dr["L0_ffn_W1"][e * 1024 + i * 128: e * 1024 + (i + 1) * 128, :],
                           sem="wa0", waits=gw1 if i == 0 else None)
            gw2 = [w2_end[e - 1]] if e >= 1 else None
            for h in range(16):
                wb = gload(BIG2[:, 16384 + h * 1024: 16384 + (h + 1) * 1024],
                           dr["L0_ffn_W2"][e * 2048 + h * 128: e * 2048 + (h + 1) * 128, :],
                           sem="wb0", waits=gw2 if h == 0 else None)
            b2w[e] = gload(B2E[0:1, :], dr["L0_ffn_b2"][e:e + 1, :], sem="wb1",
                           waits=gw2)
            for g0 in range(2):
                tpt = P.bank_wait("PT")
                for j in range(8):
                    tc = g0 * 8 + j
                    wts = (tpt if j == 0 else []) + \
                          ([gmt_done] if (g0 == 0 and j == 0 and e == 0) else [])
                    P.emit("t", "transpose", PT[0:1, j * 128:(j + 1) * 128],
                           GMT[:, tc * 8 + e: tc * 8 + e + 1], CS["idb"][:, :],
                           waits=wts, inc="ts")
                P.emit("v", "tensor_copy", GMR[0:1, g0 * 1024:(g0 + 1) * 1024],
                       PT[0:1, 0:1024], wait=P.w("ts"), inc="vs")
                P.emit("v", "drain")
                P.bank_used("PT", P.w("vs"))
            gmr_w = P.w("vs")

            for tt in range(4):
                ts0 = tt * 512
                b6 = 2
                tw6 = P.bank_wait(b6)
                P.emit("t", "matmul", PSN[b6][:, 0:512], CS["ones1b"][:, :],
                       GMR[0:1, ts0:ts0 + 512], start=True, stop=True,
                       waits=tw6 + [gmr_w], inc="ts")
                P.emit("v", "tensor_copy", BC[:, :], PSN[b6][:, 0:512],
                       wait=P.w("ts"), inc="vs")
                P.emit("v", "drain")
                P.bank_used(b6, P.w("vs"))
                bc_w = P.w("vs")
                for m in range(16):
                    bk = m % 4
                    twk = P.bank_wait(bk)
                    for i in range(8):
                        ws = (twk if i == 0 else [])
                        if m == 0 and i == 0 and tt == 0:
                            ws = ws + [wa, lnx_all]
                        P.emit("t", "matmul", PSN[bk][:, 0:512],
                               BIG2[:, i * 2048 + m * 128: i * 2048 + m * 128 + 128],
                               lnx(i, ts0, 512), start=(i == 0), stop=(i == 7),
                               waits=ws, inc="ts" if i == 7 else None)
                    tsw = P.w("ts")
                    P.emit("s", "activation", hmsl(m), PSN[bk][:, 0:512], AF.Relu,
                           bias=B1C[:, e * 16 + m: e * 16 + m + 1],
                           waits=[tsw], inc="ss")
                    P.emit("s", "drain")
                    P.emit("v", "tensor_mul", hmsl(m), hmsl(m), BC[:, :],
                           waits=[P.w("ss"), bc_w] if m == 0 else [P.w("ss")],
                           inc="vs")
                    P.emit("v", "drain")
                    P.bank_used(bk, P.w("vs"))
                hm_w = P.w("vs")
                if tt == 3:
                    h_end[e] = P.w("ts")
                for dd in range(8):
                    bk = dd % 4
                    twk = P.bank_wait(bk)
                    for h in range(16):
                        ws = (twk if h == 0 else [])
                        if dd == 0 and h == 0:
                            ws = ws + [hm_w]
                            if tt == 0:
                                ws = ws + [wb, b2w[e]]
                        P.emit("t", "matmul", PSN[bk][:, 0:512],
                               BIG2[:, 16384 + h * 1024 + dd * 128:
                                    16384 + h * 1024 + dd * 128 + 128],
                               hmsl(h), start=(h == 0), stop=False,
                               waits=ws)
                    P.emit("t", "matmul", PSN[bk][:, 0:512],
                           B2E[0:1, dd * 128:(dd + 1) * 128],
                           GMR[0:1, ts0:ts0 + 512], start=False, stop=True,
                           inc="ts")
                    tsw = P.w("ts")
                    if e == 0:
                        P.emit("v", "tensor_copy", ysl(dd, ts0), PSN[bk][:, 0:512],
                               waits=[tsw], inc="vs")
                    else:
                        P.emit("v", "tensor_tensor", ysl(dd, ts0), PSN[bk][:, 0:512],
                               ysl(dd, ts0), ALU.add, waits=[tsw], inc="vs")
                    P.emit("v", "drain")
                    P.bank_used(bk, P.w("vs"))
                if tt == 3:
                    w2_end[e] = P.w("ts")
                if e == 7:
                    ye7[tt] = P.w("vs")
        y_done = P.w("vs")

        rsfree_d = rs_free
        for tt in range(4):
            ts0 = tt * 512
            wR = hdma(RS[:, :],
                      bass.AP(src_dram, ts0, [[2048, 128], [128 * 2048, 8], [1, 512]]),
                      waits=rsfree_d if tt == 0 else [prev_store])
            for i in range(8):
                P.emit("v", "tensor_tensor", Rsl(i), ysl(i, ts0), Rsl(i), ALU.add,
                       waits=[wR, ye7[tt]] if i == 0 else [],
                       inc="vs" if i == 7 else None)
                P.emit("v", "drain")
            prev_store = hdma(bass.AP(dst_dram, ts0,
                                      [[2048, 128], [128 * 2048, 8], [1, 512]]),
                              RS[:, :], waits=[P.w("vs")], sem="hfs")
        for wi, wname in enumerate(("L1_attn_w_down_q", "L1_attn_w_down_kv")):
            for r in range(4):
                c0 = 10880 + (wi * 4 + r) * 1024
                gload(BIG3[:, c0:c0 + 1024], dr[wname][r * 128:(r + 1) * 128, :],
                      sem="wpre")
        PRELD["dn"] = P.w("wpre")
        P.barrier()
        return prev_store

    def block_l1_attn(src_dram, dst_dram, entry_store=None):
        def kres(i):
            return BIG1[:, i * 2048:(i + 1) * 2048]

        def qres(i):
            return BIG1[:, 16384 + i * 2048: 16384 + (i + 1) * 2048]

        def outsb(i):
            return BIG2[:, 16640 + i * 512: 16640 + (i + 1) * 512]

        def pslot(i):
            return BIG2[:, 20736 + i * 512: 20736 + (i + 1) * 512]

        def hsl(i):
            return BIG2[:, 22784 + i * 512: 22784 + (i + 1) * 512]

        def cq(i):
            return BIG2[:, 26880 + i * 512: 26880 + (i + 1) * 512]

        def ckv(i):
            return BIG2[:, 28928 + i * 512: 28928 + (i + 1) * 512]

        RSC1 = BIG2[:, 30976:31488]
        RSC2 = BIG2[:, 31488:32000]
        b2t = BIG2.tensor if hasattr(BIG2, "tensor") else BIG2

        natb = [BIG2[:, 0:1024], BIG2[:, 1024:2048]]
        if "dn" in PRELD:
            pq = (lambda r: BIG3[:, 10880 + r * 1024: 10880 + (r + 1) * 1024],
                  PRELD["dn"])
            pk = (lambda r: BIG3[:, 14976 + r * 1024: 14976 + (r + 1) * 1024],
                  PRELD["dn"])
            prep_wT(dr["L1_attn_w_down_q"], 512, D, BIG3, 0, None, pre=pq)
            prep_wT(dr["L1_attn_w_down_kv"], 512, D, BIG3, 4096, None, evac="s",
                    pre=pk)
        else:
            prep_wT(dr["L1_attn_w_down_q"], 512, D, BIG3, 0, natb)
            prep_wT(dr["L1_attn_w_down_kv"], 512, D, BIG3, 4096, natb, evac="s")
        prep_wT(dr["L1_attn_w_up_q"], D, 512, BIG3, 8192, natb)
        prep_wT(dr["L1_attn_w_up_k"], D, 512, BIG3, 12288, natb, evac="s")
        prep_wT(dr["L1_attn_w_up_v"], D, 512, BIG3, 16384, natb)
        prep_done = [P.w("vs"), P.w("ss")]
        P.emit("t", None, waits=prep_done)
        P.emit("v", "memset",
               bass.AP(b2t, 64, [[32768, 128], [1040, 16], [65, 16]]), 1.0,
               waits=prep_done, inc="msn", incn=16)
        vone_w = P.w("msn")

        def wdq(i, mc):
            return BIG3[:, i * 512 + mc * 128: i * 512 + mc * 128 + 128]

        def wdkv(i, mc):
            return BIG3[:, 4096 + i * 512 + mc * 128: 4096 + i * 512 + mc * 128 + 128]

        def wup(base, c, m, w=128):
            return BIG3[:, base + c * 1024 + m * w: base + c * 1024 + (m + 1) * w]

        QM = QKM[0:16, 0:1]
        KM = QKM[0:16, 1:2]
        RS2R = STAT[0:1, 2048:2560]
        pfA = {}
        diA, mmA = ln_prefetch(src_dram, 0, 0, rs2_row=RS2R, entry=entry_store)
        diA()
        pfA[0] = mmA()

        def SAP(t0):
            return bass.AP(src_dram, t0, [[2048, 128], [128 * 2048, 8], [1, 512]])

        def DAP(t0):
            return bass.AP(dst_dram, t0, [[2048, 128], [128 * 2048, 8], [1, 512]])

        def apply_l1(tt, wRx, extra_sw=None):
            return ln_apply(tt % 2, lambda i: Rsl(i), None, pfA[tt],
                            src_wait=[wRx] + (extra_sw or []),
                            add_to=lambda i: Rsl(i),
                            rs2_row=RS2R, dst2=lambda i: hsl(i),
                            gb2=(lambda i: pv("l1s", i), lambda i: pv("l1b", i)),
                            banks=((4, PSN[4]), (3, PSN[3]), ("PT", PTF0)))

        wR = hdma(RS[:, :], SAP(0), waits=[entry_store] if entry_store else None)
        dhw, dw = apply_l1(0, wR)
        prev_store = hdma(DAP(0), RS[:, :], waits=[dw], sem="hes")
        rope_used = None
        phaseA_last_dp = None
        for tt in range(4):
            ts0 = tt * 512
            wr2a = gload(ROPE2[:, 0:512], dr["ropeC"][:, ts0:ts0 + 512], sem="wr2",
                         waits=list(rope_used) if rope_used else None)
            wr2 = gload(ROPE2[:, 512:1024], dr["ropeS"][:, ts0:ts0 + 512], sem="wr2")
            if tt < 3:
                diA2, mmA2 = ln_prefetch(src_dram, (tt + 1) * 512, (tt + 1) % 2,
                                         rs2_row=RS2R, entry=entry_store)
                diA2()
            def dp(wfn, dst, extra=None):
                for mc in range(4):
                    bk = mc % 2
                    tw = P.bank_wait(bk)
                    for i in range(8):
                        ws = (tw if i == 0 else [])
                        if mc == 0 and i == 0:
                            ws = ws + [dhw] + (extra or [])
                        P.emit("t", "matmul", PSN[bk][:, 0:512], wfn(i, mc),
                               hsl(i), start=(i == 0), stop=(i == 7),
                               waits=ws, inc="ts" if i == 7 else None)
                    P.emit("s", "activation", dst(mc), PSN[bk][:, 0:512],
                           AF.Identity, waits=[P.w("ts")], inc="ss")
                    P.emit("s", "drain")
                    P.bank_used(bk, P.w("ss"))
            dp(wdq, cq)
            dp(wdkv, ckv)
            cqw = P.w("ss")
            phaseA_last_dp = P.w("ts")
            hsl_free = P.w("ts")
            if tt < 3:
                pfA[tt + 1] = mmA2(row_free=[ln_apply.last_bc])

            def up(base, src_fn, dst_fn, do_rope, norm_dst):
                for m in range(8):
                    bk = m % 2
                    tw = P.bank_wait(bk)
                    for c in range(4):
                        ws = (tw if c == 0 else [])
                        if m == 0 and c == 0:
                            ws = ws + [cqw]
                        P.emit("t", "matmul", PSN[bk][:, 0:512],
                               wup(base, c, m), src_fn(c),
                               start=(c == 0), stop=(c == 3),
                               waits=ws, inc="ts" if c == 3 else None)
                    P.emit("s", "activation", dst_fn(m)[:, ts0:ts0 + 512],
                           PSN[bk][:, 0:512], AF.Identity,
                           waits=[P.w("ts")], inc="ss")
                    P.emit("s", "drain")
                    P.bank_used(bk, P.w("ss"))
                    evw = P.w("ss")
                    if do_rope:
                        bk2 = 2
                        tw2 = P.bank_wait(bk2)
                        P.emit("t", "matmul", PSN[bk2][:, 0:512], CS["swapb"][:, :],
                               dst_fn(m)[:, ts0:ts0 + 512], start=True, stop=True,
                               waits=tw2 + [evw] +
                                     ([wr2, vone_w] if (m == 0 and tt >= 0) else []),
                               inc="ts")
                        swp = P.w("ts")
                        P.emit("g", "tensor_mul", RSC1,
                               dst_fn(m)[:, ts0:ts0 + 512], ROPE2[:, 0:512],
                               waits=[evw] + ([wr2] if m == 0 else []), inc="gs")
                        gw = P.w("gs")
                        P.emit("v", "tensor_tensor", RSC2, PSN[bk2][:, 0:512],
                               ROPE2[:, 512:1024], ALU.mult, waits=[swp])
                        P.emit("v", "drain")
                        P.emit("v", "engine_nop", inc="vs")
                        P.bank_used(bk2, P.w("vs"))
                        P.emit("v", "tensor_add", dst_fn(m)[:, ts0:ts0 + 512],
                               RSC1, RSC2, waits=[gw], inc="vs")
                        P.emit("v", "drain")
                        rw = P.w("vs")
                        P.emit("s", "activation", RSC2, dst_fn(m)[:, ts0:ts0 + 512],
                               AF.Square, waits=[rw], inc="ss")
                        P.emit("s", "drain")
                        bk5 = 5
                        tw5 = P.bank_wait(bk5) if m == 0 else []
                        P.emit("t", "matmul", PSN[bk5][0:16, 0:512],
                               CS["e16"][:, m * 16:(m + 1) * 16], RSC2,
                               start=(m == 0), stop=(m == 7),
                               waits=[P.w("ss")] + tw5,
                               inc="ts" if m == 7 else None)
                nw = P.w("ts")
                P.emit("v", "reduce_max", QKM[0:16, 2:3], PSN[5][0:16, 0:512],
                       AX.X, waits=[nw])
                P.emit("v", "drain")
                if tt == 0:
                    P.emit("v", "tensor_copy", norm_dst, QKM[0:16, 2:3], inc="vs")
                else:
                    P.emit("v", "tensor_max", norm_dst, norm_dst,
                           QKM[0:16, 2:3], inc="vs")
                P.emit("v", "drain")
                P.bank_used(5, P.w("vs"))

            up(8192, cq, qres, True, QM)
            up(12288, ckv, kres, True, KM)
            rope_used = [P.w("gs"), P.w("vs")]

            if tt < 3:
                wR2 = hdma(RS[:, :], SAP(ts0 + 512), waits=[dw, prev_store])
                dhw, dw = apply_l1(tt + 1, wR2, extra_sw=[hsl_free])
                prev_store = hdma(DAP(ts0 + 512), RS[:, :], waits=[dw],
                                  sem="hes")

            for c in range(4):
                tc = tt * 4 + c
                for oh in range(2):
                    bk = oh % 2
                    tw = P.bank_wait(bk)
                    for ic in range(4):
                        ws = (tw if ic == 0 else [])
                        P.emit("t", "matmul", PSN[bk][:, 0:512],
                               ckv(ic)[:, c * 128:(c + 1) * 128],
                               BIG3[:, 16384 + ic * 1024 + oh * 512:
                                    16384 + ic * 1024 + (oh + 1) * 512],
                               start=(ic == 0), stop=(ic == 3),
                               waits=ws, inc="ts" if ic == 3 else None)
                    P.emit("v", "tensor_copy",
                           bass.AP(b2t, tc * 1040 + oh * 520,
                                   [[32768, 128], [65, 8], [1, 64]]),
                           PSN[bk][:, 0:512], waits=[P.w("ts")], inc="vs")
                    P.emit("v", "drain")
                    P.bank_used(bk, P.w("vs"))
        vaug_w = P.w("vs")

        natb2 = [BIG3[:, 0:1024], BIG3[:, 1024:2048]]
        P.emit("g", None, wait=phaseA_last_dp)
        wout_prep = prep_wT(dr["L1_attn_w_out"], D, D, BIG2, 22784, natb2)

        P.emit("v", "tensor_tensor", QKM[0:16, 3:4], QM, KM, ALU.mult, inc="vs")
        P.emit("v", "drain")
        P.emit("s", "activation", QKM[0:16, 3:4], QKM[0:16, 3:4], AF.Sqrt,
               wait=P.w("vs"), inc="ss")
        P.emit("s", "drain")
        P.emit("v", "tensor_scalar_mul", QKM[0:16, 3:4], QKM[0:16, 3:4],
               -1.05 / 8.0, wait=P.w("ss"), inc="vs")
        P.emit("v", "drain")
        bk6 = 2
        tw6 = P.bank_wait(bk6)
        P.emit("t", "transpose", PSN[bk6][0:1, 0:16], QKM[0:16, 3:4],
               CS["idf"][0:16, 0:16], waits=tw6 + [P.w("vs")], inc="ts")
        P.emit("v", "tensor_copy", STAT[0:1, 1024:1040], PSN[bk6][0:1, 0:16],
               wait=P.w("ts"), inc="vs")
        P.emit("v", "drain")
        P.bank_used(bk6, P.w("vs"))
        for h in range(16):
            tw6 = P.bank_wait(bk6)
            P.emit("t", "matmul", PSN[bk6][:, 0:1], CS["onesall"][0:1, :],
                   STAT[0:1, 1024 + h:1025 + h], start=True, stop=True,
                   waits=tw6 + ([P.w("vs")] if h == 0 else []), inc="ts")
            P.emit("v", "tensor_copy", SCRF[:, h:h + 1], PSN[bk6][:, 0:1],
                   wait=P.w("ts"), inc="vs")
            P.emit("v", "drain")
            P.bank_used(bk6, P.w("vs"))
        bias_w = P.w("vs")

        LIN = SCRF[64:65, 512:1024]
        LBC = SCRF[0:64, 512:1024]
        PTF = PT.bitcast(F32)
        for tq in range(4):
            ts0 = tq * 512

            LIN2 = STAT[64:65, 0:512]

            def norm_pair(hp, av_done):
                hA, hB = 2 * hp, 2 * hp + 1
                P.emit("v", "reciprocal", LIN, PSN[4][64:65, 0:512],
                       waits=[av_done])
                P.emit("v", "drain")
                P.emit("v", "reciprocal", LIN2, PSN[5][64:65, 0:512], inc="vs")
                P.emit("v", "drain")
                lw = P.w("vs")
                for oi, (hh, ob, lrow) in enumerate(((hA, 4, LIN),
                                                    (hB, 5, LIN2))):
                    twp = P.bank_wait("PT")
                    P.emit("t", "matmul", PTF[0:64, 0:512],
                           CS["onesall"][64:65, 0:64], lrow,
                           start=True, stop=True,
                           waits=twp + ([lw] if oi == 0 else []), inc="ts")
                    P.emit("v", "tensor_copy", LBC, PTF[0:64, 0:512],
                           wait=P.w("ts"))
                    P.emit("v", "drain")
                    P.emit("v", "engine_nop", inc="vs")
                    P.bank_used("PT", P.w("vs"))
                    P.emit("v", "tensor_tensor",
                           outsb(hp)[(hh % 2) * 64:(hh % 2) * 64 + 64, :],
                           PSN[ob][0:64, 0:512], LBC, ALU.mult, inc="vs")
                    P.emit("v", "drain")
                    P.bank_used(ob, P.w("vs"))

            pend = None
            for hp in range(8):
                hA, hB = 2 * hp, 2 * hp + 1
                pw = {}
                sw = {}

                def s_pair(tk, w0=None):
                    sA, sB = (2 * tk) % 4, (2 * tk + 1) % 4
                    twA = P.bank_wait(sA)
                    twB = P.bank_wait(sB)
                    P.emit("t", "matmul", PSN[sA][:, 0:512],
                           kres(hp)[0:64, tk * 128:(tk + 1) * 128],
                           qres(hp)[0:64, ts0:ts0 + 512], start=True, stop=True,
                           tile_position=(0, 0), waits=twA + (w0 or []), inc="ts")
                    sw[(tk, 0)] = P.w("ts")
                    P.emit("t", "matmul", PSN[sB][:, 0:512],
                           kres(hp)[64:128, tk * 128:(tk + 1) * 128],
                           qres(hp)[64:128, ts0:ts0 + 512], start=True, stop=True,
                           tile_position=(64, 0), waits=twB, inc="ts")
                    sw[(tk, 1)] = P.w("ts")

                w0 = [bias_w, vaug_w, wout_prep] if (tq == 0 and hp == 0) else None
                s_pair(0, w0)
                s_pair(1)
                if pend is not None:
                    norm_pair(*pend)
                    pend = None
                for tk in range(16):
                    if 1 <= tk < 15:
                        s_pair(tk + 1)
                    sA, sB = (2 * tk) % 4, (2 * tk + 1) % 4
                    pA, pB = tk % 2, 2 + tk % 2
                    P.emit("s", "activation", pslot(pA), PSN[sA][:, 0:512], AF.Exp,
                           scale=0.125, bias=SCRF[:, hA:hA + 1],
                           waits=[sw[(tk, 0)]] + ([pw[tk - 2]] if tk >= 2 else []),
                           inc="ss")
                    P.bank_used(sA, P.w("ss"))
                    eAw = P.w("ss")
                    P.emit("s", "activation", pslot(pB), PSN[sB][:, 0:512], AF.Exp,
                           scale=0.125, bias=SCRF[:, hB:hB + 1],
                           waits=[sw[(tk, 1)]], inc="ss")
                    P.bank_used(sB, P.w("ss"))
                    eBw = P.w("ss")
                    avw = P.bank_wait(4) if tk == 0 else []
                    P.emit("t", "matmul", PSN[4][0:65, 0:512],
                           bass.AP(b2t, tk * 1040 + hA * 65, [[32768, 128], [1, 65]]),
                           pslot(pA), start=(tk == 0), stop=(tk == 15),
                           waits=[eAw] + avw, inc="ts")
                    avw = P.bank_wait(5) if tk == 0 else []
                    P.emit("t", "matmul", PSN[5][0:65, 0:512],
                           bass.AP(b2t, tk * 1040 + hB * 65, [[32768, 128], [1, 65]]),
                           pslot(pB), start=(tk == 0), stop=(tk == 15),
                           waits=[eBw] + avw, inc="ts")
                    pw[tk] = P.w("ts")
                pend = (hp, P.w("ts"))
            norm_pair(*pend)
            osb_w = P.w("vs")
            wR2 = hdma(RS[:, :],
                       bass.AP(dst_dram, ts0, [[2048, 128], [128 * 2048, 8], [1, 512]]),
                       waits=[prev_store] if tq == 0 else [prev_store])
            for m in range(8):
                bk = m % 2
                tw = P.bank_wait(bk)
                for i in range(8):
                    ws = (tw if i == 0 else [])
                    if m == 0 and i == 0:
                        ws = ws + [osb_w]
                    P.emit("t", "matmul", PSN[bk][:, 0:512],
                           BIG2[:, 22784 + i * 1024 + m * 128:
                                22784 + i * 1024 + m * 128 + 128],
                           outsb(i), start=(i == 0), stop=(i == 7),
                           waits=ws, inc="ts" if i == 7 else None)
                P.emit("v", "tensor_tensor", Rsl(m), PSN[bk][:, 0:512], Rsl(m),
                       ALU.add, waits=[P.w("ts")] + ([wR2] if m == 0 else []),
                       inc="vs")
                P.emit("v", "drain")
                P.bank_used(bk, P.w("vs"))
            prev_store = hdma(bass.AP(dst_dram, ts0,
                                      [[2048, 128], [128 * 2048, 8], [1, 512]]),
                              RS[:, :], waits=[P.w("vs")], sem="hfs")
        P.barrier()
        return prev_store

    def block_l1_ffn(src_dram, dst_dram, entry_store=None):
        def usl(m, th=None):
            b = m * 1024
            if th is None:
                return BIG1[:, b:b + 1024]
            return BIG1[:, b + th * 512: b + th * 512 + 512]

        def g1sl(m, th=None):
            b = m * 1024
            if th is None:
                return BIG2[:, b:b + 1024]
            return BIG2[:, b + th * 512: b + th * 512 + 512]

        def lnx4(i, th):
            return BIG3[:, i * 1024 + th * 512: i * 1024 + th * 512 + 512]

        NA = [BIG3[:, 8192:9216], BIG3[:, 9216:10240]]
        NB = [BIG3[:, 10240:11264], BIG3[:, 11264:12288]]
        NH = [BIG3[:, 8192:10240], BIG3[:, 10240:12288]]
        TRS = [12288, 16384]

        prev_store = entry_store
        rs_free = []
        hists = {"a": [], "b": [], "h": [], "y": []}
        pfF = {}
        for th in range(2):
            diF, mmF = ln_prefetch(src_dram, th * 512, th, entry=entry_store,
                                   banks=((4, PSN[4]), (5, PSN[5])))
            diF()
            pfF[(0, th)] = mmF(row_free=[ln_apply.last_bc]
                               if hasattr(ln_apply, "last_bc") else None)
        for tt in range(2):
            base = tt * 1024
            lnw = []
            for th in range(2):
                ts0 = base + th * 512
                wR = hdma(RS[:, :],
                          bass.AP(src_dram, ts0,
                                  [[2048, 128], [128 * 2048, 8], [1, 512]]),
                          waits=(([prev_store] if prev_store else []) + rs_free)
                          or None)
                dw, _ = ln_apply(th, lambda i: Rsl(i),
                                 lambda i, th=th: lnx4(i, th),
                                 pfF[(tt, th)], src_wait=wR)
                lnw.append(dw)
                rs_free = [P.w("vs")]
            if tt == 0:
                diF0, mmF0 = ln_prefetch(src_dram, 1024, 0, entry=entry_store,
                                         banks=((4, PSN[4]), (5, PSN[5])))
                diF0()
            wmk = P.w("ts")
            hists["a"] += [wmk, wmk]
            hists["b"] += [wmk, wmk]
            for m in range(32):
                tro = TRS[m % 2]
                w1w = prep_wT(dr["L1_ffn_w1_w"][m * 128:(m + 1) * 128, :], 128, 1024,
                              BIG3, tro, NA, hist=hists["a"],
                              nslots=2, sem=("wa0", "wa1"), deep=True)
                w2w = prep_wT(dr["L1_ffn_w2_w"][m * 128:(m + 1) * 128, :], 128, 1024,
                              BIG3, tro + 1024, NB,
                              hist=hists["b"], nslots=2, sem=("wb0", "wb1"),
                              evac="s", deep=True)
                for th in range(2):
                    bA, bB = th % 2, 2 + th % 2
                    twA = P.bank_wait(bA)
                    twB = P.bank_wait(bB)
                    for i in range(8):
                        ws = (twA if i == 0 else [])
                        if i == 0 and th == 0:
                            ws = ws + [w1w] + ([lnw[-1]] if m == 0 else [])
                        P.emit("t", "matmul", PSN[bA][:, 0:512],
                               BIG3[:, tro + i * 128: tro + (i + 1) * 128],
                               lnx4(i, th), start=(i == 0), stop=(i == 7),
                               waits=ws, inc="ts" if i == 7 else None)
                    aw = P.w("ts")
                    for i in range(8):
                        ws = (twB if i == 0 else [])
                        if i == 0 and th == 0:
                            ws = ws + [w2w]
                        P.emit("t", "matmul", PSN[bB][:, 0:512],
                               BIG3[:, tro + 1024 + i * 128: tro + 1024 + (i + 1) * 128],
                               lnx4(i, th), start=(i == 0), stop=(i == 7),
                               waits=ws, inc="ts" if i == 7 else None)
                    bw = P.w("ts")
                    P.emit("s", "activation", g1sl(m, th), PSN[bA][:, 0:512],
                           AF.Silu, scale=scol("L1_ffn_w1_s", m),
                           waits=[aw], inc="ss")
                    P.emit("s", "drain")
                    P.bank_used(bA, P.w("ss"))
                    P.emit("v", "scalar_tensor_tensor", usl(m, th), PSN[bB][:, 0:512],
                           scol("L1_ffn_w2_s", m), g1sl(m, th), ALU.mult, ALU.mult,
                           waits=[bw, P.w("ss")], inc="vs")
                    P.emit("v", "drain")
                    P.bank_used(bB, P.w("vs"))
            u_done = P.w("vs")
            if tt == 0:
                pfF[(1, 0)] = mmF0(row_free=[ln_apply.last_bc])
                diF1, mmF1 = ln_prefetch(src_dram, 1024 + 512, 1,
                                         entry=entry_store,
                                         banks=((4, PSN[4]), (5, PSN[5])))
                diF1()
                pfF[(1, 1)] = mmF1(row_free=[ln_apply.last_bc])

            def hidden_layer(wname, sname, src_sl, dst_sl, first_w):
                wmk2 = P.w("ts")
                hists["h"] += [wmk2, wmk2]
                for m in range(32):
                    tro = TRS[m % 2]
                    hwA = prep_wT(dr[wname][m * 128:(m + 1) * 128, 0:2048],
                                  128, 2048, BIG3, tro, NH, hist=hists["h"],
                                  nslots=2, sem=("wb0", "wb1"), deep=True)
                    gA = list(prep_wT.last_groups)
                    hwB = prep_wT(dr[wname][m * 128:(m + 1) * 128, 2048:4096],
                                  128, 2048, BIG3, tro + 2048, NH,
                                  hist=hists["h"], nslots=2, sem=("wb0", "wb1"),
                                  deep=True)
                    gB = list(prep_wT.last_groups)
                    gk = {0: gA[0], 8: gA[1], 16: gB[0], 24: gB[1]}
                    for th in range(2):
                        bk = th % 2
                        tw = P.bank_wait(bk)
                        for k in range(32):
                            ws = (tw if k == 0 else [])
                            if th == 0 and k in gk:
                                ws = ws + [gk[k]] +                                      ([first_w] if (m == 0 and k == 0) else [])
                            P.emit("t", "matmul", PSN[bk][:, 0:512],
                                   BIG3[:, tro + k * 128: tro + (k + 1) * 128],
                                   src_sl(k, th), start=(k == 0), stop=(k == 31),
                                   waits=ws, inc="ts" if k == 31 else None)
                        P.emit("s", "activation", dst_sl(m, th), PSN[bk][:, 0:512],
                               AF.Gelu, scale=scol(sname, m),
                               waits=[P.w("ts")], inc="ss")
                        P.emit("s", "drain")
                        P.bank_used(bk, P.w("ss"))
                return P.w("ss")

            g1_done = hidden_layer("L1_ffn_h1_w", "L1_ffn_h1_s",
                                   usl, g1sl, u_done)
            g2_done = hidden_layer("L1_ffn_h2_w", "L1_ffn_h2_s",
                                   g1sl, usl, g1_done)

            G1F = BIG2.bitcast(F32)
            for m in range(8):
                tro = TRS[m % 2]
                w3wA = prep_wT(dr["L1_ffn_w3_w"][m * 128:(m + 1) * 128, 0:2048],
                               128, 2048, BIG3, tro, NH, hist=hists["h"],
                               nslots=2, sem=("wb0", "wb1"), deep=True)
                g3A = list(prep_wT.last_groups)
                w3wB = prep_wT(dr["L1_ffn_w3_w"][m * 128:(m + 1) * 128,
                                                 2048:4096],
                               128, 2048, BIG3, tro + 2048, NH,
                               hist=hists["h"], nslots=2, sem=("wb0", "wb1"),
                               deep=True)
                g3B = list(prep_wT.last_groups)
                g3k = {0: g3A[0], 8: g3A[1], 16: g3B[0], 24: g3B[1]}
                for th in range(2):
                    bk = th % 2
                    tw = P.bank_wait(bk)
                    for k in range(32):
                        ws = (tw if k == 0 else [])
                        if th == 0 and k in g3k:
                            ws = ws + [g3k[k]] +                                  ([g2_done] if (m == 0 and k == 0) else [])
                        P.emit("t", "matmul", PSN[bk][:, 0:512],
                               BIG3[:, tro + k * 128: tro + (k + 1) * 128],
                               usl(k, th), start=(k == 0), stop=(k == 31),
                               waits=ws, inc="ts" if k == 31 else None)
                    P.emit("s", "activation",
                           G1F[:, th * 4096 + m * 512: th * 4096 + (m + 1) * 512],
                           PSN[bk][:, 0:512], AF.Identity,
                           scale=scol("L1_ffn_w3_s", m), waits=[P.w("ts")],
                           inc="ss")
                    P.emit("s", "drain")
                    P.bank_used(bk, P.w("ss"))
            yph_w = P.w("ss")
            for th in range(2):
                ts0 = base + th * 512
                wR = hdma(RS[:, :],
                          bass.AP(src_dram, ts0,
                                  [[2048, 128], [128 * 2048, 8], [1, 512]]),
                          waits=rs_free + ([prev_store] if prev_store else []))
                for m in range(8):
                    P.emit("v", "tensor_tensor", Rsl(m),
                           G1F[:, th * 4096 + m * 512: th * 4096 + (m + 1) * 512],
                           Rsl(m), ALU.add,
                           waits=[wR, yph_w] if m == 0 else [], inc="vs")
                    P.emit("v", "drain")
                prev_store = hdma(bass.AP(dst_dram, ts0,
                                          [[2048, 128], [128 * 2048, 8], [1, 512]]),
                                  RS[:, :], waits=[P.w("vs")], sem="hfs")
                rs_free = []
        P.barrier()
        return prev_store

    BLOCK_FN = {}
    BLOCK_FN[0] = block_l0_attn
    BLOCK_FN[1] = block_l0_ffn
    BLOCK_FN[2] = block_l1_attn
    BLOCK_FN[3] = block_l1_ffn

    srcs = [dr["xT"], dr["R"], dr["R"], dr["R"]]
    dsts = [dr["R"], dr["R"], dr["R"], dr["R"]]
    dsts[NBLOCKS - 1] = dr["out"]

    last_store = None
    for bi in range(NBLOCKS):
        last_store = BLOCK_FN[bi](srcs[bi], dsts[bi], last_store)

    P.emit("y", None, wait=P.w("hio"))
    P.barrier()
    P.build()
    return nc


_NC_CACHE = {}


def _get_nc():
    key = NBLOCKS
    if key not in _NC_CACHE:
        _NC_CACHE[key] = build_program()
    return _NC_CACHE[key]


def make_in_maps(inputs):
    base = {}
    for n, sh in WSHAPES.items():
        base[n] = np.ascontiguousarray(np.asarray(inputs[n], np.float32).reshape(sh))
    for n, ln in VSHAPES.items():
        base[n] = np.ascontiguousarray(np.asarray(inputs[n], np.float32).reshape(ln))
    for n, a in CONSTS.items():
        base[n] = a
    x = np.asarray(inputs["L0_attn_x"], np.float32)
    maps = []
    base = {("c_" + n if n in CONSTS else n): v for n, v in base.items()}
    for c in range(NCORE):
        m = dict(base)
        m["xT"] = np.ascontiguousarray(x[c].T)
        maps.append(m)
    return maps


def kernel(**inputs):
    nc = _get_nc()
    maps = make_in_maps(inputs)
    res = run_bass_kernel_spmd(nc, maps, core_ids=list(range(NCORE)))
    out = np.stack([np.asarray(r["outT"], np.float32).T for r in res.results], axis=0)
    return out
